# version 01 (final)
# baseline (speedup 1.0000x reference)
import numpy as np
import ml_dtypes
from contextlib import ExitStack
import concourse.bass as bass
import concourse.mybir as mybir
from concourse.bass_utils import run_bass_kernel_spmd

_BF = ml_dtypes.bfloat16


def _mk_A():

    F32 = mybir.dt.float32
    BF16 = mybir.dt.bfloat16
    AF = mybir.ActivationFunctionType
    ALU = mybir.AluOpType
    AX = mybir.AxisListType

    T = 2048
    TH = T + 6
    D = 1024
    KC = 8
    DF = 4096


    class S:

        def __init__(self, es, nc, name):
            self.h = es.enter_context(nc.semaphore(name))
            self.v = 0

        def inc(self, instr, n=1):
            instr.then_inc(self.h, n)
            self.v += n

        def base(self):
            return self.v

        def wait_abs(self, eng, val):
            eng.wait_ge(self.h, val)


    def build_A(debug_outs=False):
        nc = bass.Bass("TRN2", target_bir_lowering=False)

        def din(name, shape, dt=F32):
            return nc.dram_tensor(name, shape, dt, kind="ExternalInput")

        xT = din("xT", [D, T]); xhalo = din("xhalo", [6, D]); cdc = din("cdc", [128, 1])
        w1c = din("w1c", [128, 8]); w2c = din("w2c", [128, 8])
        bdc = din("bdc", [128, 8]); bgc = din("bgc", [128, 8])
        bcc = din("bcc", [128, 8]); b1c = din("b1c", [128, 8])
        sscv = din("sscv", [128, 1]); mscv = din("mscv", [128, 1])
        cwc = din("cwc", [128, 56])
        Wd = din("Wd", [D, D]); Wgt = din("Wgt", [D, D])
        Wkk = din("Wkk", [D, 256]); MTd = din("MTd", [256, 256])
        Wm = din("Wm", [256, D]); Wcc = din("Wcc", [3 * D, D])
        Wgg = din("Wgg", [D, DF]); Wup = din("Wup", [D, DF]); Wdn = din("Wdn", [DF, D])
        WfcT = din("WfcT", [D, DF]); WpjT = din("WpjT", [DF, D])
        W1h = din("W1h", [D, D])

        x1f = nc.dram_tensor("x1f", [128, KC * T], F32, kind="ExternalOutput")
        xn1f = nc.dram_tensor("xn1f", [128, KC * T], BF16, kind="ExternalOutput")
        basef = nc.dram_tensor("basef", [128, KC * T], BF16, kind="ExternalOutput")

        def stash_t(name, mt, kt):
            return nc.dram_tensor(name, [128, mt * kt * 128], BF16)
        wd_s = stash_t("wd_s", 8, 8); wg_s = stash_t("wg_s", 8, 8)
        wk_s = stash_t("wk_s", 2, 8); mt_s = stash_t("mt_s", 2, 2)
        wm_s = stash_t("wm_s", 8, 2); wc_s = stash_t("wc_s", 8, 24)
        wgg_s = stash_t("wgg_s", 32, 8); wup_s = stash_t("wup_s", 32, 8)
        wdn_s = stash_t("wdn_s", 8, 32); wfc_s = stash_t("wfc_s", 32, 8)
        wpj_s = stash_t("wpj_s", 8, 32); w1h_s = stash_t("w1h_s", 8, 8)

        h_dr = nc.dram_tensor("h_dr", [128, KC * TH], BF16)
        xn_dr = nc.dram_tensor("xn_dr", [128, KC * T], BF16)
        loc_dr = nc.dram_tensor("loc_dr", [128, KC * T], BF16)
        st_dr = nc.dram_tensor("st_dr", [128, KC * T], BF16)
        mem_dr = nc.dram_tensor("mem_dr", [128, KC * T], BF16)
        k_dr = nc.dram_tensor("k_dr", [128, 2 * T], BF16)
        mo_dr = nc.dram_tensor("mo_dr", [128, 2 * T], BF16)
        mix_dr = nc.dram_tensor("mix_dr", [128, KC * T], F32)

        es = ExitStack()

        def sb(name, shape, dt):
            return es.enter_context(nc.sbuf_tensor(name, shape, dt))

        rbc = sb("rbc", [128, T], F32); mrbc = sb("mrbc", [128, T], F32)
        rrhbc = sb("rrhbc", [128, T], F32); mrrhbc = sb("mrrhbc", [128, T], F32)
        xc0 = sb("xc0", [128, T], F32); xc1 = sb("xc1", [128, T], F32)
        rsl0 = sb("rsl0", [128, TH], BF16); rsl1 = sb("rsl1", [128, TH], BF16)
        cbf0 = sb("cbf0", [128, T], BF16); cbf1 = sb("cbf1", [128, T], BF16)
        wsl = [sb(f"wsl{i}", [128, 8 * 128], BF16) for i in range(4)]
        TA = sb("TA", [128, T + 1], F32); TB = sb("TB", [128, T + 1], F32)
        TC = sb("TC", [128, T + 1], F32)
        ev0 = sb("ev0", [128, T], F32); ev1 = sb("ev1", [128, T], F32)
        planeA = sb("planeA", [128, KC * T], BF16)
        midb = sb("midb", [128, 32 * 512], BF16)
        dg2 = sb("dg2", [128, 14 * 128], BF16)
        w1s = sb("w1s", [128, 8], F32); w2s = sb("w2s", [128, 8], F32)
        bds = sb("bds", [128, 8], F32); bgs = sb("bgs", [128, 8], F32)
        bcs = sb("bcs", [128, 8], F32); b1s = sb("b1s", [128, 8], F32)
        sscs = sb("sscs", [128, 1], F32); mscs = sb("mscs", [128, 1], F32)
        cdss = sb("cdss", [128, 1], F32); cwcs = sb("cwcs", [128, 56], F32)
        identf = sb("identf", [128, 128], F32)
        iotp = sb("iotp", [128, 1], F32)
        ones1 = sb("ones1", [1, 128], F32)
        onesc = sb("onesc", [128, 1], BF16)
        R1 = TA[0:1, 0:T]
        R2 = TB[0:1, 0:T]
        R3 = TC[0:1, 0:T]
        R4 = ev0[0:1, 0:T]
        hrow = sb("hrow", [6, 8], F32)
        hfmh = sb("hfmh", [128, 8 * 6], BF16)
        xhal = ev0[0:6, 0:D]
        hhal = ev1[0:6, 0:D]
        hhalb = None
        spkA = TA[:, 0:512]; spkB = TA[:, 512:1024]
        smem = TA[:, 1024:1536]; sacc = TA[:, 1536:2048]
        ypc = TB[:, 0:512]; gpc = TB[:, 512:1024]
        rlb = TB[:, 1024:1536]

        def ones_for(i):
            return ones1[0:1, :]

        hhalb = cbf0[0:6, 0:D]
        psA = es.enter_context(nc.psum_tensor("psA", [128, 1024], F32))
        psB = es.enter_context(nc.psum_tensor("psB", [128, 1024], F32))
        psC = es.enter_context(nc.psum_tensor("psC", [128, 1024], F32))
        psD = es.enter_context(nc.psum_tensor("psD", [128, 1024], F32))
        PS = [psA, psB, psC, psD]

        dl = S(es, nc, "dl")
        dls = S(es, nc, "dls")
        mmS = S(es, nc, "mmS")
        evS = S(es, nc, "evS")
        vcS = S(es, nc, "vcS")
        gpS = S(es, nc, "gpS")

        def stash_slab_ap(stash, mt, kt, m):
            return stash.ap().rearrange("p (m k c) -> p m k c", m=mt, k=kt)[:, m, :, :]

        def stash_kslab_ap(stash, mt, kt, k):
            return stash.ap().rearrange("p (k m c) -> p k m c", k=kt, m=mt)[:, k, :, :]

        def dr_slab(drt, width, c, w0=None, w1=None):
            ap = drt.ap().rearrange("p (c w) -> p c w", w=width)[:, c, :]
            if w0 is not None:
                ap = ap[:, w0:w1]
            return ap

        def xchunk_ap(c):
            return xT.ap().rearrange("(k p) t -> p k t", p=128)[:, c, :]

        with nc.Block() as b:
            @b.gpsimd
            def _(g):
                for t_, src in [(w1s, w1c), (w2s, w2c), (bds, bdc), (bgs, bgc), (bcs, bcc),
                                (b1s, b1c), (sscs, sscv), (mscs, mscv), (cdss, cdc), (cwcs, cwc)]:
                    dl.inc(g.dma_start(out=t_[:, :], in_=src.ap()), 16)
                dl.inc(g.dma_start(out=xhal, in_=xhalo.ap()), 16)
                g.iota(identf[:, :], [[1, 128]], channel_multiplier=0,
                       allow_small_or_imprecise_dtypes=True)
                g.iota(iotp[:, :1], [[1, 1]], channel_multiplier=1,
                       allow_small_or_imprecise_dtypes=True)
                g.memset(ones1[:, :], 1.0)
                g.memset(onesc[:, :], 1.0)
                for dst, src, mt, kt in [
                        (wd_s, Wd, 8, 8), (wg_s, Wgt, 8, 8), (wk_s, Wkk, 2, 8),
                        (mt_s, MTd, 2, 2), (wm_s, Wm, 8, 2), (wc_s, Wcc, 8, 24),
                        (wgg_s, Wgg, 32, 8), (wup_s, Wup, 32, 8),
                        (wfc_s, WfcT, 32, 8), (w1h_s, W1h, 8, 8)]:
                    do = dst.ap().rearrange("p (m k c) -> p m k c", m=mt, k=kt)
                    si = src.ap().rearrange("(k p) (m c) -> p m k c", p=128, c=128)
                    step = mt if 128 * mt * kt <= 8192 else 1
                    for m0_ in range(0, mt, step):
                        dl.inc(g.dma_start(out=do[:, m0_:m0_ + step, :, :],
                                           in_=si[:, m0_:m0_ + step, :, :]), 16)
                for dst, src, mt, kt in [(wdn_s, Wdn, 8, 32), (wpj_s, WpjT, 8, 32)]:
                    do = dst.ap().rearrange("p (k m c) -> p k m c", k=kt, m=mt)
                    si = src.ap().rearrange("(k p) (m c) -> p k m c", p=128, c=128)
                    for k0_ in range(0, kt):
                        dl.inc(g.dma_start(out=do[:, k0_:k0_ + 1, :, :],
                                           in_=si[:, k0_:k0_ + 1, :, :]), 16)
                g.wait_ge(dl.h, dl.v)

        with nc.Block() as b:
            @b.vector
            def _(v):
                v.tensor_scalar(identf[:, :], identf[:, :], iotp[:, :1], None, ALU.is_equal)

        with nc.Block() as b:
            d0, e0, m0 = dl.base(), evS.base(), mmS.base()

            @b.gpsimd
            def _(g):
                for c in range(KC):
                    if c >= 2:
                        g.wait_ge(mmS.h, m0 + (c - 1))
                    dl.inc(g.dma_start(out=(cbf0 if c % 2 == 0 else cbf1)[:, :],
                                       in_=xchunk_ap(c)), 16)

            @b.scalar
            def _(s):
                for c in range(KC):
                    s.wait_ge(dl.h, d0 + 16 * (c + 1))
                    if c >= 2:
                        s.wait_ge(mmS.h, m0 + (c - 1))
                    src = cbf0 if c % 2 == 0 else cbf1
                    dst = rsl0 if c % 2 == 0 else rsl1
                    evS.inc(s.activation(dst[:, 0:T], src[:, :], AF.Square), 1)

            @b.tensor
            def _(t):
                for c in range(KC):
                    t.wait_ge(evS.h, e0 + (c + 1))
                    xb = cbf0 if c % 2 == 0 else cbf1
                    sq = (rsl0 if c % 2 == 0 else rsl1)[:, 0:T]
                    for nt in range(4):
                        pt = [psA, psA, psB, psB][nt]
                        t.matmul(pt[0:1, (nt % 2) * 512:(nt % 2) * 512 + 512],
                                 onesc[:, :], xb[:, nt * 512:(nt + 1) * 512],
                                 start=(c == 0), stop=(c == KC - 1))
                    ins = None
                    for nt in range(4):
                        pt = [psC, psC, psD, psD][nt]
                        ins = t.matmul(pt[0:1, (nt % 2) * 512:(nt % 2) * 512 + 512],
                                       onesc[:, :], sq[:, nt * 512:(nt + 1) * 512],
                                       start=(c == 0), stop=(c == KC - 1))
                    mmS.inc(ins, 1)

        with nc.Block() as b:
            v0, e0 = vcS.base(), evS.base()

            @b.vector
            def _(v):
                v.tensor_scalar(R1[0:1, 0:1024], psA[0:1, :], 1.0 / D, None, ALU.mult)
                v.tensor_scalar(R1[0:1, 1024:2048], psB[0:1, :], 1.0 / D, None, ALU.mult)
                v.tensor_scalar(R2[0:1, 0:1024], psC[0:1, :], 1.0 / D, None, ALU.mult)
                v.tensor_scalar(R2[0:1, 1024:2048], psD[0:1, :], 1.0 / D, None, ALU.mult)
                v.scalar_tensor_tensor(R3[0:1, :], R1[0:1, :], 1.0, R1[0:1, :], ALU.mult, ALU.mult)
                v.tensor_sub(R3[0:1, :], R2[0:1, :], R3[0:1, :])
                ins = v.tensor_scalar(R2[0:1, :], R3[0:1, :], 1e-5, None, ALU.add)
                vcS.inc(ins, 1)
                v.wait_ge(evS.h, e0 + 1)
                v.reciprocal(R2[0:1, :], R2[0:1, :])
                v.tensor_mul(R2[0:1, :], R3[0:1, :], R2[0:1, :])
                ins = v.tensor_scalar(R2[0:1, :], R2[0:1, :], 1e-6, None, ALU.add)
                vcS.inc(ins, 1)
                v.tensor_mul(R1[0:1, :], R1[0:1, :], R4[0:1, :])
                v.wait_ge(evS.h, e0 + 2)
                v.tensor_mul(R3[0:1, :], R4[0:1, :], R2[0:1, :])
                ins = v.tensor_mul(R2[0:1, :], R1[0:1, :], R2[0:1, :])
                vcS.inc(ins, 1)

            @b.scalar
            def _(s):
                s.wait_ge(vcS.h, v0 + 1)
                s.activation(R4[0:1, :], R2[0:1, :], AF.Ln)
                evS.inc(s.activation(R4[0:1, :], R4[0:1, :], AF.Exp, scale=-0.5), 1)
                s.wait_ge(vcS.h, v0 + 2)
                s.activation(R2[0:1, :], R2[0:1, :], AF.Ln)
                evS.inc(s.activation(R2[0:1, :], R2[0:1, :], AF.Exp, scale=-0.5), 1)

        with nc.Block() as b:
            m0 = mmS.base()

            @b.tensor
            def _(t):
                for half in range(2):
                    for i, (rr, dst) in enumerate([(R4, psA), (R1, psB), (R3, psC), (R2, psD)]):
                        ins = None
                        for nt in range(2):
                            off = half * 1024 + nt * 512
                            ins = t.matmul(dst[:, nt * 512:(nt + 1) * 512], ones1[0:1, :],
                                           rr[0:1, off:off + 512], start=True, stop=True)
                        mmS.inc(ins, 1)
                    if half == 0:
                        t.wait_ge(vcS.h, vcS.v + 1)

            @b.vector
            def _(v):
                for half in range(2):
                    v.wait_ge(mmS.h, m0 + 4 * (half + 1))
                    for src, dst in [(psA, rbc), (psB, mrbc), (psC, rrhbc), (psD, mrrhbc)]:
                        ins = v.tensor_copy(dst[:, half * 1024:half * 1024 + 1024], src[:, :])
                    vcS.inc(ins, 1)

        with nc.Block() as b:
            v0, e0 = vcS.base(), evS.base()

            @b.scalar
            def _(s):
                evS.inc(s.activation(hhal, xhal, AF.Square), 1)
                s.wait_ge(vcS.h, v0 + 1)
                s.activation(hrow[:, 4:5], hrow[:, 3:4], AF.Ln)
                evS.inc(s.activation(hrow[:, 4:5], hrow[:, 4:5], AF.Exp, scale=-0.5), 1)
                s.wait_ge(vcS.h, v0 + 2)
                s.activation(hrow[:, 6:7], hrow[:, 5:6], AF.Ln)
                evS.inc(s.activation(hrow[:, 6:7], hrow[:, 6:7], AF.Exp, scale=-0.5), 1)

            @b.vector
            def _(v):
                v.reduce_sum(hrow[:, 0:1], xhal, AX.X)
                v.wait_ge(evS.h, e0 + 1)
                v.reduce_sum(hrow[:, 1:2], hhal, AX.X)
                v.tensor_scalar(hrow[:, 0:1], hrow[:, 0:1], 1.0 / D, None, ALU.mult)
                v.tensor_scalar(hrow[:, 1:2], hrow[:, 1:2], 1.0 / D, None, ALU.mult)
                v.scalar_tensor_tensor(hrow[:, 2:3], hrow[:, 0:1], 1.0, hrow[:, 0:1],
                                       ALU.mult, ALU.mult)
                v.tensor_sub(hrow[:, 2:3], hrow[:, 1:2], hrow[:, 2:3])
                ins = v.tensor_scalar(hrow[:, 3:4], hrow[:, 2:3], 1e-5, None, ALU.add)
                vcS.inc(ins, 1)
                v.wait_ge(evS.h, e0 + 2)
                v.reciprocal(hrow[:, 5:6], hrow[:, 3:4])
                v.tensor_mul(hrow[:, 5:6], hrow[:, 2:3], hrow[:, 5:6])
                ins = v.tensor_scalar(hrow[:, 5:6], hrow[:, 5:6], 1e-6, None, ALU.add)
                vcS.inc(ins, 1)
                v.wait_ge(evS.h, e0 + 3)
                v.tensor_mul(hrow[:, 6:7], hrow[:, 6:7], hrow[:, 4:5])
                v.tensor_scalar(hhal, xhal, hrow[:, 0:1], None, ALU.subtract)
                ins = v.tensor_scalar(hhal, hhal, hrow[:, 6:7], None, ALU.mult)
                vcS.inc(ins, 1)

        with nc.Block() as b:
            m0, v0 = mmS.base(), vcS.base()

            @b.tensor
            def _(t):
                for c in range(KC):
                    if c >= 2:
                        t.wait_ge(vcS.h, v0 + (c - 1))
                    ins = t.transpose((psA if c % 2 == 0 else psB)[:, 0:6],
                                      ev1[0:6, c * 128:(c + 1) * 128], identf[0:6, 0:6])
                    mmS.inc(ins, 1)

            @b.vector
            def _(v):
                for c in range(KC):
                    v.wait_ge(mmS.h, m0 + c + 1)
                    src = psA if c % 2 == 0 else psB
                    ins = v.tensor_scalar(hfmh[:, c * 6:(c + 1) * 6], src[:, 0:6],
                                          w1s[:, c:c + 1], None, ALU.mult)
                    vcS.inc(ins, 1)

            @b.gpsimd
            def _(g):
                g.wait_ge(vcS.h, v0 + KC)
                dl.inc(g.dma_start(
                    out=h_dr.ap().rearrange("p (c w) -> p c w", w=TH)[:, :, 0:6],
                    in_=hfmh[:, :].rearrange("p (c w) -> p c w", w=6),
                    allow_slow_non_contiguous=True), 16)
                g.wait_ge(dl.h, dl.v)

        with nc.Block() as b:
            s0, v0, g0 = dls.base(), vcS.base(), dl.base()

            @b.sync
            def _(sy):
                for c in range(KC):
                    if c >= 2:
                        sy.wait_ge(vcS.h, v0 + 2 * (c - 1))
                    dls.inc(sy.dma_start(out=(xc0 if c % 2 == 0 else xc1)[:, :],
                                         in_=xchunk_ap(c)), 16)

            @b.vector
            def _(v):
                for c in range(KC):
                    v.wait_ge(dls.h, s0 + 16 * (c + 1))
                    if c >= 2:
                        v.wait_ge(dl.h, g0 + 32 * (c - 1))
                    xc = xc0 if c % 2 == 0 else xc1
                    v.tensor_mul(TA[:, 0:T], xc[:, :], rbc[:, :])
                    ins = v.tensor_sub(cbf0[:, :], TA[:, 0:T], mrbc[:, :])
                    vcS.inc(ins, 1)
                    v.scalar_tensor_tensor(TB[:, 0:T], xc[:, :], w1s[:, c:c + 1], rrhbc[:, :],
                                           ALU.mult, ALU.mult)
                    v.tensor_scalar(TC[:, 0:T], mrrhbc[:, :], w1s[:, c:c + 1], None, ALU.mult)
                    ins = v.tensor_sub(cbf1[:, :], TB[:, 0:T], TC[:, 0:T])
                    vcS.inc(ins, 1)

            @b.gpsimd
            def _(g):
                for c in range(KC):
                    g.wait_ge(vcS.h, v0 + 2 * c + 1)
                    dl.inc(g.dma_start(out=dr_slab(xn_dr, T, c), in_=cbf0[:, :]), 16)
                    g.wait_ge(vcS.h, v0 + 2 * c + 2)
                    dl.inc(g.dma_start(out=dr_slab(h_dr, TH, c, 6, TH), in_=cbf1[:, :]), 16)
                g.wait_ge(dl.h, dl.v)

        with nc.Block() as b:
            s0, m0, v0, g0 = dls.base(), mmS.base(), vcS.base(), dl.base()

            @b.sync
            def _(sy):
                for c in range(KC):
                    if c >= 2:
                        sy.wait_ge(mmS.h, m0 + 2 * (c - 1))
                    dls.inc(sy.dma_start(out=(rsl0 if c % 2 == 0 else rsl1)[:, :],
                                         in_=dr_slab(h_dr, TH, c)), 16)

            @b.tensor
            def _(t):
                for c in range(KC):
                    t.wait_ge(dls.h, s0 + 16 * (c + 1))
                    t.wait_ge(vcS.h, v0 + (1 if c == 0 else (2 if c == 1 else 3 * c - 1)))
                    if c >= 2:
                        t.wait_ge(vcS.h, v0 + 3 * (c - 1) + 1)
                    hs = rsl0 if c % 2 == 0 else rsl1
                    pair = (psA, psB) if c % 2 == 0 else (psC, psD)
                    for half in range(2):
                        ins = None
                        for nt in range(2):
                            n0 = half * 1024 + nt * 512
                            for tap in range(7):
                                i = (c % 2) * 7 + tap
                                ins = t.matmul(pair[half][:, nt * 512:(nt + 1) * 512],
                                               dg2[:, i * 128:(i + 1) * 128],
                                               hs[:, tap + n0: tap + n0 + 512],
                                               start=(tap == 0), stop=(tap == 6))
                        mmS.inc(ins, 1)

            @b.vector
            def _(v):
                for cb in range(2):
                    for tap in range(7):
                        i = cb * 7 + tap
                        ins = v.tensor_scalar(dg2[:, i * 128:(i + 1) * 128], identf[:, :],
                                              cwcs[:, (cb * 7 + tap):(cb * 7 + tap) + 1],
                                              None, ALU.mult)
                    vcS.inc(ins, 1)
                for c in range(KC):
                    pair = (psA, psB) if c % 2 == 0 else (psC, psD)
                    dst = cbf0 if c % 2 == 0 else cbf1
                    if c >= 2:
                        v.wait_ge(dl.h, g0 + 16 * (c - 1))
                    for half in range(2):
                        v.wait_ge(mmS.h, m0 + 2 * c + half + 1)
                        ins = v.tensor_copy(dst[:, half * 1024:half * 1024 + 1024],
                                            pair[half][:, :])
                        vcS.inc(ins, 1)
                    if c + 2 < KC:
                        v.wait_ge(mmS.h, m0 + 2 * c + 2)
                        for tap in range(7):
                            i = (c % 2) * 7 + tap
                            ins = v.tensor_scalar(dg2[:, i * 128:(i + 1) * 128], identf[:, :],
                                                  cwcs[:, ((c + 2) * 7 + tap):((c + 2) * 7 + tap) + 1],
                                                  None, ALU.mult)
                        vcS.inc(ins, 1)

            @b.gpsimd
            def _(g):
                for c in range(KC):
                    if c <= 5:
                        g.wait_ge(vcS.h, v0 + 2 + 3 * c + 2)
                    else:
                        g.wait_ge(vcS.h, v0 + 20 + 2 * (c - 6) + 2)
                    dl.inc(g.dma_start(out=dr_slab(loc_dr, T, c),
                                       in_=(cbf0 if c % 2 == 0 else cbf1)[:, :]), 16)
                g.wait_ge(dl.h, dl.v)


        with nc.Block() as b:
            g0, s0, m0, e0, v0, gp0 = (dl.base(), dls.base(), mmS.base(), evS.base(),
                                       vcS.base(), gpS.base())
            NM = KC
            wl_done, st_done = {}, {}
            dv = g0
            for m in range(NM):
                dv += 32
                wl_done[m] = dv
                if m >= 1:
                    dv += 16
                    st_done[m - 1] = dv
            dv += 16
            st_done[NM - 1] = dv

            @b.gpsimd
            def _(g):
                def chain(mq):
                    g.wait_ge(vcS.h, v0 + 6 * mq + 3)
                    gpS.inc(g.tensor_tensor(TB[:, 0:T], ev1[:, :], TB[:, 0:T], ALU.mult), 1)
                    g.wait_ge(vcS.h, v0 + 6 * mq + 4)
                    gpS.inc(g.tensor_tensor(TB[:, 0:T], TB[:, 0:T], TC[:, 0:T], ALU.mult), 1)
                    g.wait_ge(vcS.h, v0 + 6 * mq + 6)
                    dl.inc(g.dma_start(out=dr_slab(st_dr, T, mq),
                                       in_=(cbf0 if mq % 2 == 0 else cbf1)[:, :]), 16)
                for m in range(NM):
                    if m >= 2:
                        g.wait_ge(mmS.h, m0 + 16 * (m - 1))
                    dl.inc(g.dma_start(out=wsl[2 * (m % 2)][:, :],
                                       in_=stash_slab_ap(wd_s, 8, 8, m)), 16)
                    dl.inc(g.dma_start(out=wsl[2 * (m % 2) + 1][:, :],
                                       in_=stash_slab_ap(wg_s, 8, 8, m)), 16)
                    if m >= 1:
                        chain(m - 1)
                chain(NM - 1)
                g.wait_ge(dl.h, dl.v)

            @b.sync
            def _(sy):
                for m in range(NM):
                    for k in range(KC):
                        idx = m * KC + k
                        if idx >= 2:
                            sy.wait_ge(mmS.h, m0 + 2 * (idx - 1))
                        dls.inc(sy.dma_start(out=(rsl0 if idx % 2 == 0 else rsl1)[:, :],
                                             in_=dr_slab(h_dr, TH, k)), 16)
                    if m >= 2:
                        sy.wait_ge(vcS.h, v0 + 6 * (m - 2) + 2)
                    dls.inc(sy.dma_start(out=(xc0 if m % 2 == 0 else xc1)[:, :],
                                         in_=xchunk_ap(m)), 16)

            @b.tensor
            def _(t):
                for m in range(NM):
                    t.wait_ge(dl.h, wl_done[m])
                    if m >= 1:
                        t.wait_ge(evS.h, e0 + 2 * m)
                    wd_sl = wsl[2 * (m % 2)]
                    wg_sl = wsl[2 * (m % 2) + 1]
                    for k in range(KC):
                        idx = m * KC + k
                        t.wait_ge(dls.h, s0 + 16 * (m * (KC + 1) + k + 1))
                        rs = (rsl0 if idx % 2 == 0 else rsl1)
                        ins = None
                        for half in range(2):
                            for nt in range(2):
                                n0 = 6 + half * 1024 + nt * 512
                                t.matmul((psA, psB)[half][:, nt * 512:(nt + 1) * 512],
                                         wd_sl[:, k * 128:(k + 1) * 128],
                                         rs[:, n0:n0 + 512], start=(k == 0), stop=(k == KC - 1),
                                         skip_group_check=True)
                                ins = t.matmul((psC, psD)[half][:, nt * 512:(nt + 1) * 512],
                                               wg_sl[:, k * 128:(k + 1) * 128],
                                               rs[:, n0:n0 + 512], start=(k == 0),
                                               stop=(k == KC - 1), skip_group_check=True)
                        mmS.inc(ins, 2)

            @b.scalar
            def _(s):
                for m in range(NM):
                    s.wait_ge(mmS.h, m0 + 16 * (m + 1))
                    if m >= 1:
                        s.wait_ge(vcS.h, v0 + 6 * (m - 1) + 1)
                        s.wait_ge(gpS.h, gp0 + 2 * (m - 1) + 1)
                    s.activation(ev0[:, 0:1024], psA[:, :], AF.Sigmoid, bias=bds[:, m:m + 1])
                    ins = s.activation(ev0[:, 1024:2048], psB[:, :], AF.Sigmoid, bias=bds[:, m:m + 1])
                    evS.inc(ins, 1)
                    s.activation(ev1[:, 0:1024], psC[:, :], AF.Sigmoid, bias=bgs[:, m:m + 1])
                    ins = s.activation(ev1[:, 1024:2048], psD[:, :], AF.Sigmoid, bias=bgs[:, m:m + 1])
                    evS.inc(ins, 1)

            @b.vector
            def _(v):
                for m in range(NM):
                    v.wait_ge(evS.h, e0 + 2 * m + 1)
                    v.tensor_scalar(ev0[:, :], ev0[:, :], 0.95, 0.05, ALU.min, ALU.max)
                    v.tensor_copy(TA[:, 0:1], cdss[:, :])
                    ins = v.tensor_tensor_scan(TA[:, 1:T + 1], ev0[:, :], ev0[:, :],
                                               cdss[:, 0:1], ALU.mult, ALU.bypass)
                    vcS.inc(ins, 1)
                    v.wait_ge(dls.h, s0 + 16 * ((m + 1) * (KC + 1)))
                    if m >= 1:
                        v.wait_ge(gpS.h, gp0 + 2 * m)
                    xc = xc0 if m % 2 == 0 else xc1
                    v.scalar_tensor_tensor(TB[:, 0:T], xc[:, :], w1s[:, m:m + 1], rrhbc[:, :],
                                           ALU.mult, ALU.mult)
                    v.tensor_scalar(TC[:, 0:T], mrrhbc[:, :], w1s[:, m:m + 1], None, ALU.mult)
                    ins = v.tensor_sub(TB[:, 0:T], TB[:, 0:T], TC[:, 0:T])
                    vcS.inc(ins, 1)
                    v.wait_ge(evS.h, e0 + 2 * m + 2)
                    vcS.inc(v.engine_nop(), 1)
                    v.tensor_scalar(TC[:, 0:T], TA[:, 0:T], 1e-7, None, ALU.max)
                    ins = v.reciprocal(TC[:, 0:T], TC[:, 0:T])
                    vcS.inc(ins, 1)
                    v.wait_ge(gpS.h, gp0 + 2 * m + 2)
                    ins = v.tensor_tensor_scan(TC[:, 0:T], TB[:, 0:T], TB[:, 0:T],
                                               0.0, ALU.add, ALU.bypass)
                    vcS.inc(ins, 1)
                    if m >= 2:
                        v.wait_ge(dl.h, st_done[m - 2])
                    ins = v.scalar_tensor_tensor((cbf0 if m % 2 == 0 else cbf1)[:, :],
                                                 TC[:, 0:T], sscs[:, 0:1],
                                                 TA[:, 1:T + 1], ALU.mult, ALU.mult)
                    vcS.inc(ins, 1)

        def gemm_simple(stash, MT_, KT_, rhs_fn, store_fn, evac, width=T, note=""):
            with nc.Block() as b:
                g0, s0, m0, e0, v0 = dl.base(), dls.base(), mmS.base(), evS.base(), vcS.base()
                wl_done, st_done = {}, {}
                dv = g0
                for m in range(MT_):
                    dv += 16
                    wl_done[m] = dv
                    if m >= 1:
                        dv += 16
                        st_done[m - 1] = dv
                dv += 16
                st_done[MT_ - 1] = dv
                ev_eng = 'scalar' if evac[0] == 'scalar_bias' else 'vector'

                @b.gpsimd
                def _(g):
                    for m in range(MT_):
                        if m >= 2:
                            g.wait_ge(mmS.h, m0 + 2 * KT_ * (m - 1))
                        dl.inc(g.dma_start(out=wsl[m % 2][:, 0:KT_ * 128],
                                           in_=stash_slab_ap(stash, MT_, KT_, m)), 16)
                        if m >= 1:
                            if ev_eng == 'vector':
                                g.wait_ge(vcS.h, v0 + m)
                            else:
                                g.wait_ge(evS.h, e0 + m)
                            dl.inc(g.dma_start(out=store_fn(m - 1),
                                               in_=(cbf0 if (m - 1) % 2 == 0 else cbf1)[:, :]), 16)
                    if ev_eng == 'vector':
                        g.wait_ge(vcS.h, v0 + MT_)
                    else:
                        g.wait_ge(evS.h, e0 + MT_)
                    dl.inc(g.dma_start(out=store_fn(MT_ - 1),
                                       in_=(cbf0 if (MT_ - 1) % 2 == 0 else cbf1)[:, :]), 16)
                    g.wait_ge(dl.h, dl.v)

                @b.sync
                def _(sy):
                    for m in range(MT_):
                        for k in range(KT_):
                            idx = m * KT_ + k
                            if idx >= 2:
                                sy.wait_ge(mmS.h, m0 + 2 * (idx - 1))
                            dls.inc(sy.dma_start(out=(rsl0 if idx % 2 == 0 else rsl1)[:, 0:T],
                                                 in_=rhs_fn(k)), 16)

                @b.tensor
                def _(t):
                    for m in range(MT_):
                        t.wait_ge(dl.h, wl_done[m])
                        if m >= 2:
                            if ev_eng == 'vector':
                                t.wait_ge(vcS.h, v0 + m - 1)
                            else:
                                t.wait_ge(evS.h, e0 + m - 1)
                        for k in range(KT_):
                            idx = m * KT_ + k
                            t.wait_ge(dls.h, s0 + 16 * (idx + 1))
                            rs = (rsl0 if idx % 2 == 0 else rsl1)
                            ins = None
                            for half in range(2):
                                for nt in range(2):
                                    n0 = half * 1024 + nt * 512
                                    ins = t.matmul((psA, psB)[half][:, nt * 512:(nt + 1) * 512]
                                                   if m % 2 == 0 else
                                                   (psC, psD)[half][:, nt * 512:(nt + 1) * 512],
                                                   wsl[m % 2][:, k * 128:(k + 1) * 128],
                                                   rs[:, n0:n0 + 512], start=(k == 0),
                                                   stop=(k == KT_ - 1), skip_group_check=True)
                            mmS.inc(ins, 2)

                if ev_eng == 'vector':
                    @b.vector
                    def _(v):
                        for m in range(MT_):
                            v.wait_ge(mmS.h, m0 + 2 * KT_ * (m + 1))
                            if m >= 2:
                                v.wait_ge(dl.h, st_done[m - 2])
                            pair = (psA, psB) if m % 2 == 0 else (psC, psD)
                            dst = cbf0 if m % 2 == 0 else cbf1
                            ins = None
                            for half in range(2):
                                sl = dst[:, half * 1024:half * 1024 + 1024]
                                if evac[0] == 'copy':
                                    ins = v.tensor_copy(sl, pair[half][:, :])
                                elif evac[0] == 'scale_col':
                                    ins = v.tensor_scalar(sl, pair[half][:, :], evac[1][:, 0:1],
                                                          None, ALU.mult)
                            vcS.inc(ins, 1)
                else:
                    @b.scalar
                    def _(s):
                        for m in range(MT_):
                            s.wait_ge(mmS.h, m0 + 2 * KT_ * (m + 1))
                            if m >= 2:
                                s.wait_ge(dl.h, st_done[m - 2])
                            pair = (psA, psB) if m % 2 == 0 else (psC, psD)
                            dst = cbf0 if m % 2 == 0 else cbf1
                            ins = None
                            for half in range(2):
                                sl = dst[:, half * 1024:half * 1024 + 1024]
                                ins = s.activation(sl, pair[half][:, :], AF.Copy,
                                                   bias=evac[1][:, m:m + 1])
                            evS.inc(ins, 1)

        gemm_simple(wk_s, 2, 8, lambda k: dr_slab(h_dr, TH, k, 6, TH),
                    lambda m: dr_slab(k_dr, T, m), ('copy',))
        gemm_simple(mt_s, 2, 2, lambda k: dr_slab(k_dr, T, k),
                    lambda m: dr_slab(mo_dr, T, m), ('copy',))
        gemm_simple(wm_s, 8, 2, lambda k: dr_slab(mo_dr, T, k),
                    lambda m: dr_slab(mem_dr, T, m), ('scale_col', mscs))

        with nc.Block() as b:
            g0, s0, m0, e0, v0 = dl.base(), dls.base(), mmS.base(), evS.base(), vcS.base()
            MT_, KT_ = 8, 24

            def rhs24(k):
                return dr_slab((loc_dr, st_dr, mem_dr)[k // 8], T, k % 8)

            wl_done, st_done = {}, {}
            dv = g0
            for m in range(MT_):
                dv += 48
                wl_done[m] = dv
                if m >= 1:
                    dv += 16
                    st_done[m - 1] = dv
            dv += 16
            st_done[MT_ - 1] = dv

            @b.gpsimd
            def _(g):
                for m in range(MT_):
                    if m >= 1:
                        g.wait_ge(mmS.h, m0 + 2 * KT_ * m)
                    for part in range(3):
                        dl.inc(g.dma_start(out=wsl[part][:, :],
                                           in_=stash_slab_ap(wc_s, 8, 24, m)
                                           [:, part * 8:(part + 1) * 8, :]), 16)
                    if m >= 1:
                        g.wait_ge(vcS.h, v0 + 2 * m)
                        dl.inc(g.dma_start(out=dr_slab(mix_dr, T, m - 1),
                                           in_=(ev0 if (m - 1) % 2 == 0 else ev1)[:, :]), 16)
                g.wait_ge(vcS.h, v0 + 2 * MT_)
                dl.inc(g.dma_start(out=dr_slab(mix_dr, T, MT_ - 1),
                                   in_=(ev0 if (MT_ - 1) % 2 == 0 else ev1)[:, :]), 16)
                g.wait_ge(dl.h, dl.v)

            @b.sync
            def _(sy):
                for m in range(MT_):
                    for k in range(KT_):
                        idx = m * KT_ + k
                        if idx >= 2:
                            sy.wait_ge(mmS.h, m0 + 2 * (idx - 1))
                        dls.inc(sy.dma_start(out=(rsl0 if idx % 2 == 0 else rsl1)[:, 0:T],
                                             in_=rhs24(k)), 16)
                    if m >= 2:
                        sy.wait_ge(vcS.h, v0 + 2 * (m - 1))
                    dls.inc(sy.dma_start(out=(xc0 if m % 2 == 0 else xc1)[:, :],
                                         in_=xchunk_ap(m)), 16)

            @b.tensor
            def _(t):
                for m in range(MT_):
                    t.wait_ge(dl.h, wl_done[m])
                    if m >= 2:
                        t.wait_ge(vcS.h, v0 + 2 * (m - 1))
                    for k in range(KT_):
                        idx = m * KT_ + k
                        t.wait_ge(dls.h, s0 + 16 * (m * (KT_ + 1) + k + 1))
                        rs = (rsl0 if idx % 2 == 0 else rsl1)
                        ins = None
                        for half in range(2):
                            for nt in range(2):
                                n0 = half * 1024 + nt * 512
                                ins = t.matmul((psA, psB)[half][:, nt * 512:(nt + 1) * 512]
                                               if m % 2 == 0 else
                                               (psC, psD)[half][:, nt * 512:(nt + 1) * 512],
                                               wsl[k // 8][:, (k % 8) * 128:(k % 8) * 128 + 128],
                                               rs[:, n0:n0 + 512], start=(k == 0),
                                               stop=(k == KT_ - 1), skip_group_check=True)
                        mmS.inc(ins, 2)

            @b.vector
            def _(v):
                for m in range(MT_):
                    v.wait_ge(mmS.h, m0 + 2 * KT_ * (m + 1))
                    v.wait_ge(dls.h, s0 + 16 * ((m + 1) * (KT_ + 1)))
                    if m >= 2:
                        v.wait_ge(dl.h, st_done[m - 2])
                    pair = (psA, psB) if m % 2 == 0 else (psC, psD)
                    for half in range(2):
                        v.tensor_scalar(TA[:, half * 1024:half * 1024 + 1024],
                                        pair[half][:, :], bcs[:, m:m + 1], None, ALU.add)
                    xc = xc0 if m % 2 == 0 else xc1
                    dst = ev0 if m % 2 == 0 else ev1
                    v.tensor_mul(TB[:, 0:T], xc[:, :], rbc[:, :])
                    v.tensor_sub(TB[:, 0:T], TB[:, 0:T], mrbc[:, :])
                    ins = v.tensor_add(dst[:, :], TA[:, 0:T], TB[:, 0:T])
                    vcS.inc(ins, 2)

        with nc.Block() as b:
            s0, m0, e0, v0 = dls.base(), mmS.base(), evS.base(), vcS.base()

            @b.sync
            def _(sy):
                for c in range(KC):
                    if c >= 2:
                        sy.wait_ge(evS.h, e0 + (c - 1))
                    dls.inc(sy.dma_start(out=(xc0 if c % 2 == 0 else xc1)[:, :],
                                         in_=dr_slab(mix_dr, T, c)), 16)

            @b.scalar
            def _(s):
                for c in range(KC):
                    s.wait_ge(dls.h, s0 + 16 * (c + 1))
                    if c >= 2:
                        s.wait_ge(mmS.h, m0 + (c - 1))
                    src = xc0 if c % 2 == 0 else xc1
                    dst = cbf0 if c % 2 == 0 else cbf1
                    evS.inc(s.activation(dst[:, :], src[:, :], AF.Square), 1)

            @b.tensor
            def _(t):
                for c in range(KC):
                    t.wait_ge(evS.h, e0 + (c + 1))
                    sq = cbf0 if c % 2 == 0 else cbf1
                    ins = None
                    for nt in range(4):
                        pt = [psC, psC, psD, psD][nt]
                        ins = t.matmul(pt[0:1, (nt % 2) * 512:(nt % 2) * 512 + 512],
                                       onesc[:, :], sq[:, nt * 512:(nt + 1) * 512],
                                       start=(c == 0), stop=(c == KC - 1))
                    mmS.inc(ins, 1)

        with nc.Block() as b:
            v0, e0, m0 = vcS.base(), evS.base(), mmS.base()

            @b.vector
            def _(v):
                v.tensor_scalar(R1[0:1, 0:1024], psC[0:1, :], 1.0 / D, None, ALU.mult)
                v.tensor_scalar(R1[0:1, 1024:2048], psD[0:1, :], 1.0 / D, None, ALU.mult)
                ins = v.tensor_scalar(R1[0:1, :], R1[0:1, :], 1e-6, None, ALU.add)
                vcS.inc(ins, 1)

            @b.scalar
            def _(s):
                s.wait_ge(vcS.h, v0 + 1)
                s.activation(R1[0:1, :], R1[0:1, :], AF.Ln)
                evS.inc(s.activation(R1[0:1, :], R1[0:1, :], AF.Exp, scale=-0.5), 1)

            @b.tensor
            def _(t):
                t.wait_ge(evS.h, e0 + 1)
                for half in range(2):
                    ins = None
                    for nt in range(2):
                        off = half * 1024 + nt * 512
                        ins = t.matmul(psA[:, nt * 512:(nt + 1) * 512], ones1[0:1, :],
                                       R1[0:1, off:off + 512], start=True, stop=True)
                    mmS.inc(ins, 1)
                    if half == 0:
                        t.wait_ge(vcS.h, v0 + 2)

            @b.vector
            def _(v):
                for half in range(2):
                    v.wait_ge(mmS.h, m0 + half + 1)
                    ins = v.tensor_copy(rbc[:, half * 1024:half * 1024 + 1024], psA[:, :])
                    vcS.inc(ins, 1)

        with nc.Block() as b:
            s0, v0 = dls.base(), vcS.base()

            @b.sync
            def _(sy):
                for c in range(KC):
                    if c >= 2:
                        sy.wait_ge(vcS.h, v0 + (c - 1))
                    dls.inc(sy.dma_start(out=(xc0 if c % 2 == 0 else xc1)[:, :],
                                         in_=dr_slab(mix_dr, T, c)), 16)

            @b.vector
            def _(v):
                for c in range(KC):
                    v.wait_ge(dls.h, s0 + 16 * (c + 1))
                    xc = xc0 if c % 2 == 0 else xc1
                    ins = v.scalar_tensor_tensor(planeA[:, c * T:(c + 1) * T], xc[:, :],
                                                 w2s[:, c:c + 1], rbc[:, :], ALU.mult, ALU.mult)
                    vcS.inc(ins, 1)

        def big_ffn_block(stash_g, stash_u, stash_d, act_kind, out_phase):
            with nc.Block() as b:
                g0, s0, m0, e0, v0, gp0 = (dl.base(), dls.base(), mmS.base(), evS.base(),
                                           vcS.base(), gpS.base())
                NMID = 32
                two_w = stash_u is not None
                ld_per_tt = (2 * NMID if two_w else NMID) + NMID
                st_per_tt = 8
                dma_per_tt = ld_per_tt + st_per_tt
                if out_phase == 'spike_x1':
                    syn_per_tt = 16
                else:
                    syn_per_tt = 8
                mm_per_tt = NMID + NMID
                if out_phase == 'spike_x1':
                    vc_per_tt = NMID + 8 * 9
                    gp_per_tt = 8 * 8
                    ev_per_tt = NMID + 8
                else:
                    vc_per_tt = NMID + 8
                    gp_per_tt = 0
                    ev_per_tt = NMID if act_kind == 'swiglu' else NMID

                def dlb(tt):
                    return g0 + tt * dma_per_tt * 16

                def syb(tt):
                    return s0 + tt * syn_per_tt * 16

                def mmb(tt):
                    return m0 + tt * mm_per_tt

                def vcb(tt):
                    return v0 + tt * vc_per_tt

                def evb(tt):
                    return e0 + tt * ev_per_tt

                def gpb(tt):
                    return gp0 + tt * gp_per_tt

                @b.gpsimd
                def _(g):
                    for tt in range(4):
                        for m in range(NMID):
                            if two_w:
                                if tt > 0 and m < 2:
                                    g.wait_ge(mmS.h, mmb(tt - 1) + 2 * NMID)
                                if m >= 2:
                                    g.wait_ge(mmS.h, mmb(tt) + (m - 1))
                                dl.inc(g.dma_start(out=wsl[2 * (m % 2)][:, :],
                                                   in_=stash_slab_ap(stash_g, 32, 8, m)), 16)
                                dl.inc(g.dma_start(out=wsl[2 * (m % 2) + 1][:, :],
                                                   in_=stash_slab_ap(stash_u, 32, 8, m)), 16)
                            else:
                                if tt > 0 and m < 2:
                                    g.wait_ge(mmS.h, mmb(tt - 1) + 2 * NMID)
                                if m >= 2:
                                    g.wait_ge(mmS.h, mmb(tt) + (m - 1))
                                dl.inc(g.dma_start(out=wsl[m % 2][:, :],
                                                   in_=stash_slab_ap(stash_g, 32, 8, m)), 16)
                        for m2 in range(NMID):
                            if m2 < 4:
                                g.wait_ge(mmS.h, mmb(tt) + NMID)
                            else:
                                g.wait_ge(mmS.h, mmb(tt) + NMID + (m2 - 3))
                            dl.inc(g.dma_start(out=wsl[m2 % 4][:, 0:8 * 128],
                                               in_=stash_kslab_ap(stash_d, 8, 32, m2)), 16)
                        if out_phase == 'store_s':
                            for p8 in range(8):
                                g.wait_ge(vcS.h, vcb(tt) + NMID + p8 + 1)
                                dl.inc(g.dma_start(
                                    out=dr_slab(mix_dr, T, p8)[:, tt * 512:(tt + 1) * 512],
                                    in_=ev0[:, (p8 % 2) * 512:(p8 % 2) * 512 + 512]), 16)
                        else:
                            for p8 in range(8):
                                g.wait_ge(vcS.h, vcb(tt) + NMID + 9 * p8 + 9)
                                dl.inc(g.dma_start(
                                    out=dr_slab(x1f, T, p8)[:, tt * 512:(tt + 1) * 512],
                                    in_=ev0[:, (p8 % 2) * 512:(p8 % 2) * 512 + 512]), 16)
                    g.wait_ge(dl.h, dl.v)

                if out_phase == 'spike_x1':
                    @b.gpsimd
                    def _(g2):
                        pass

                @b.sync
                def _(sy):
                    for tt in range(4):
                        for p8 in range(8):
                            if out_phase == 'store_s':
                                if tt * 8 + p8 >= 2:
                                    pi = tt * 8 + p8 - 2
                                    sy.wait_ge(vcS.h, vcb(pi // 8) + NMID + (pi % 8) + 1)
                                dls.inc(sy.dma_start(
                                    out=(xc0 if p8 % 2 == 0 else xc1)[:, 0:512],
                                    in_=dr_slab(mix_dr, T, p8)[:, tt * 512:(tt + 1) * 512]), 16)
                            else:
                                if tt * 8 + p8 >= 2:
                                    pi = tt * 8 + p8 - 2
                                    sy.wait_ge(vcS.h, vcb(pi // 8) + NMID + 9 * (pi % 8) + 9)
                                dls.inc(sy.dma_start(
                                    out=(xc0 if p8 % 2 == 0 else xc1)[:, 0:512],
                                    in_=dr_slab(mix_dr, T, p8)[:, tt * 512:(tt + 1) * 512]), 16)
                                dls.inc(sy.dma_start(
                                    out=(xc0 if p8 % 2 == 0 else xc1)[:, 512:1024],
                                    in_=xchunk_ap(p8)[:, tt * 512:(tt + 1) * 512]), 16)

                @b.tensor
                def _(t):
                    for tt in range(4):
                        for m in range(NMID):
                            if two_w:
                                t.wait_ge(dl.h, dlb(tt) + 32 * (m + 1))
                            else:
                                t.wait_ge(dl.h, dlb(tt) + 16 * (m + 1))
                            if m < 2:
                                t.wait_ge(vcS.h, vcb(tt))
                            else:
                                t.wait_ge(vcS.h, vcb(tt) + (m - 1))
                            slot = (m % 2) * 512
                            ins = None
                            for k in range(KC):
                                rh = planeA[:, k * T + tt * 512: k * T + tt * 512 + 512]
                                if two_w:
                                    t.matmul(psA[:, slot:slot + 512],
                                             wsl[2 * (m % 2)][:, k * 128:(k + 1) * 128], rh,
                                             start=(k == 0), stop=(k == KC - 1),
                                             skip_group_check=True)
                                    ins = t.matmul(psB[:, slot:slot + 512],
                                                   wsl[2 * (m % 2) + 1][:, k * 128:(k + 1) * 128],
                                                   rh, start=(k == 0), stop=(k == KC - 1),
                                                   skip_group_check=True)
                                else:
                                    ins = t.matmul(psA[:, slot:slot + 512],
                                                   wsl[m % 2][:, k * 128:(k + 1) * 128], rh,
                                                   start=(k == 0), stop=(k == KC - 1),
                                                   skip_group_check=True)
                            mmS.inc(ins, 1)
                        t.wait_ge(vcS.h, vcb(tt) + NMID)
                        for m2 in range(NMID):
                            t.wait_ge(dl.h, dlb(tt) + 16 * (ld_per_tt - NMID + m2 + 1))
                            ins = None
                            for mg in range(4):
                                for i in range(2):
                                    mo = 2 * mg + i
                                    ins = t.matmul(PS[mg][:, i * 512:(i + 1) * 512],
                                                   wsl[m2 % 4][:, mo * 128:(mo + 1) * 128],
                                                   midb[:, m2 * 512:(m2 + 1) * 512],
                                                   start=(m2 == 0), stop=(m2 == NMID - 1),
                                                   skip_group_check=True)
                            mmS.inc(ins, 1)

                @b.scalar
                def _(s):
                    for tt in range(4):
                        for m in range(NMID):
                            s.wait_ge(mmS.h, mmb(tt) + m + 1)
                            if tt * NMID + m >= 1:
                                pm = tt * NMID + m - 1
                                s.wait_ge(vcS.h, vcb(pm // NMID) + (pm % NMID) + 1)
                            slot = (m % 2) * 512
                            fn = AF.Silu if act_kind == 'swiglu' else AF.Relu
                            evS.inc(s.activation(rlb, psA[:, slot:slot + 512], fn), 1)
                        if out_phase == 'spike_x1':
                            for p8 in range(8):
                                s.wait_ge(gpS.h, gpb(tt) + 8 * p8 + 8)
                                if tt * 8 + p8 >= 1:
                                    pi = tt * 8 + p8 - 1
                                    s.wait_ge(vcS.h, vcb(pi // 8) + NMID + 9 * (pi % 8) + 8)
                                evS.inc(s.activation(gpc, sacc, AF.Silu, scale=0.2), 1)

                @b.vector
                def _(v):
                    for tt in range(4):
                        for m in range(NMID):
                            v.wait_ge(evS.h, evb(tt) + m + 1)
                            slot = (m % 2) * 512
                            src2 = psB if two_w else psA
                            ins = v.tensor_mul(midb[:, m * 512:(m + 1) * 512], rlb,
                                               src2[:, slot:slot + 512])
                            vcS.inc(ins, 1)
                        if out_phase == 'store_s':
                            v.wait_ge(mmS.h, mmb(tt) + 2 * NMID)
                            for p8 in range(8):
                                v.wait_ge(dls.h, syb(tt) + 16 * (p8 + 1))
                                if tt * 8 + p8 >= 2:
                                    pi = tt * 8 + p8 - 2
                                    v.wait_ge(dl.h, dlb(pi // 8) + 16 * (ld_per_tt + (pi % 8) + 1))
                                mg, i = p8 // 2, p8 % 2
                                ins = v.tensor_add(ev0[:, (p8 % 2) * 512:(p8 % 2) * 512 + 512],
                                                   PS[mg][:, i * 512:(i + 1) * 512],
                                                   (xc0 if p8 % 2 == 0 else xc1)[:, 0:512])
                                vcS.inc(ins, 1)
                        else:
                            v.wait_ge(mmS.h, mmb(tt) + 2 * NMID)
                            for p8 in range(8):
                                mg, i = p8 // 2, p8 % 2
                                if tt * 8 + p8 >= 2:
                                    pi = tt * 8 + p8 - 2
                                    v.wait_ge(dl.h, dlb(pi // 8) + 16 * (ld_per_tt + (pi % 8) + 1))
                                vcS.inc(v.tensor_copy(ypc, PS[mg][:, i * 512:(i + 1) * 512]), 1)
                                vcS.inc(v.tensor_scalar(spkA, ypc, 1.0, None, ALU.is_gt), 1)
                                if tt * 8 + p8 >= 1:
                                    pi = tt * 8 + p8 - 1
                                    v.wait_ge(evS.h, evb(pi // 8) + NMID + (pi % 8) + 1)
                                v.tensor_copy(smem, ypc)
                                vcS.inc(v.tensor_copy(sacc, spkA), 1)
                                for it in range(4):
                                    v.wait_ge(gpS.h, gpb(tt) + 8 * p8 + 2 * it)
                                    v.scalar_tensor_tensor(smem, smem, 0.9, ypc,
                                                           ALU.mult, ALU.add)
                                    vcS.inc(v.engine_nop(), 1)
                                    v.wait_ge(gpS.h, gpb(tt) + 8 * p8 + 2 * it + 1)
                                    dst = spkB if it % 2 == 0 else spkA
                                    ins = v.tensor_scalar(dst[:, :], smem, 1.0, None, ALU.is_gt)
                                    vcS.inc(ins, 1)
                                v.wait_ge(evS.h, evb(tt) + NMID + p8 + 1)
                                v.wait_ge(dls.h, syb(tt) + 16 * (2 * p8 + 1))
                                v.tensor_mul(gpc, gpc,
                                             (xc0 if p8 % 2 == 0 else xc1)[:, 0:512])
                                v.wait_ge(dls.h, syb(tt) + 16 * (2 * p8 + 2))
                                ins = v.tensor_add(ev0[:, (p8 % 2) * 512:(p8 % 2) * 512 + 512],
                                                   gpc,
                                                   (xc0 if p8 % 2 == 0 else xc1)[:, 512:1024])
                                vcS.inc(ins, 1)

                if out_phase == 'spike_x1':
                    @b.vector
                    def _(v2):
                        pass

        big_ffn_block(wgg_s, wup_s, wdn_s, 'swiglu', 'store_s')

        with nc.Block() as b:
            g0, s0, m0, e0, v0, gp0 = (dl.base(), dls.base(), mmS.base(), evS.base(),
                                       vcS.base(), gpS.base())
            NMID = 32
            dl_t = {}
            dv = g0
            for c in range(KC):
                dv += 16
                dl_t[('pl', c)] = dv
            for tt in range(4):
                for m in range(NMID):
                    dv += 16
                    dl_t[('wfc', tt, m)] = dv
                for m2 in range(NMID):
                    dv += 16
                    dl_t[('wpj', tt, m2)] = dv
                for p in range(8):
                    dv += 16
                    dl_t[('st', tt, p)] = dv
            dls_t = {}
            sv = s0
            for tt in range(4):
                for p in range(8):
                    sv += 16
                    dls_t[('sp', tt, p)] = sv
                    sv += 16
                    dls_t[('xp', tt, p)] = sv
            mm_t = {}
            mv = m0
            for tt in range(4):
                for m in range(NMID):
                    mv += 1
                    mm_t[('z', tt, m)] = mv
                for m2 in range(NMID):
                    mv += 1
                    mm_t[('ff', tt, m2)] = mv
            ev_t = {}
            evv = e0
            for tt in range(4):
                for m in range(NMID):
                    evv += 1
                    ev_t[('relu', tt, m)] = evv
                for p in range(8):
                    evv += 1
                    ev_t[('silu', tt, p)] = evv
            vc_t = {}
            vv = v0
            for tt in range(4):
                for m in range(NMID):
                    vv += 1
                    vc_t[('mid', tt, m)] = vv
                for p in range(8):
                    vv += 1
                    vc_t[('ya', tt, p)] = vv
                    vv += 1
                    vc_t[('init', tt, p)] = vv
                    for it in range(4):
                        vv += 1
                        vc_t[('d', tt, p, it)] = vv
                        vv += 1
                        vc_t[('e', tt, p, it)] = vv
                    vv += 1
                    vc_t[('g', tt, p)] = vv
            gp_t = {}
            gv = gp0
            for tt in range(4):
                for p in range(8):
                    for it in range(4):
                        gv += 1
                        gp_t[('sub', tt, p, it)] = gv
                        gv += 1
                        gp_t[('add', tt, p, it)] = gv

            def prev_key(keys, tt, p, back=1):
                i = tt * 8 + p - back
                return (i // 8, i % 8) if i >= 0 else None

            @b.gpsimd
            def _(g):
                for c in range(KC):
                    dl.inc(g.dma_start(out=planeA[:, c * T:(c + 1) * T],
                                       in_=dr_slab(xn_dr, T, c)), 16)
                for tt in range(4):
                    for m in range(NMID):
                        if tt > 0 and m < 2:
                            g.wait_ge(mmS.h, mm_t[('ff', tt - 1, NMID - 1)])
                        if m >= 2:
                            g.wait_ge(mmS.h, mm_t[('z', tt, m - 2)])
                        dl.inc(g.dma_start(out=wsl[m % 2][:, :],
                                           in_=stash_slab_ap(wfc_s, 32, 8, m)), 16)
                    for m2 in range(NMID):
                        if m2 < 4:
                            g.wait_ge(mmS.h, mm_t[('z', tt, NMID - 1)])
                        else:
                            g.wait_ge(mmS.h, mm_t[('ff', tt, m2 - 4)])
                        dl.inc(g.dma_start(out=wsl[m2 % 4][:, 0:8 * 128],
                                           in_=stash_kslab_ap(wpj_s, 8, 32, m2)), 16)
                    for p in range(8):
                        for it in range(4):
                            g.wait_ge(vcS.h, vc_t[('d', tt, p, it)])
                            prevb = spkA if it == 0 else (spkB if (it - 1) % 2 == 0 else spkA)
                            gpS.inc(g.tensor_tensor(smem, smem, prevb[:, :],
                                                    ALU.subtract), 1)
                            g.wait_ge(vcS.h, vc_t[('e', tt, p, it)])
                            curb = spkB if it % 2 == 0 else spkA
                            gpS.inc(g.tensor_tensor(sacc, sacc, curb[:, :],
                                                    ALU.add), 1)
                        g.wait_ge(vcS.h, vc_t[('g', tt, p)])
                        dl.inc(g.dma_start(
                            out=dr_slab(x1f, T, p)[:, tt * 512:(tt + 1) * 512],
                            in_=ev0[:, (p % 2) * 512:(p % 2) * 512 + 512]), 16)
                g.wait_ge(dl.h, dl.v)

            @b.sync
            def _(sy):
                for tt in range(4):
                    for p in range(8):
                        pk = prev_key(None, tt, p, 2)
                        if pk is not None:
                            sy.wait_ge(vcS.h, vc_t[('g', pk[0], pk[1])])
                        dls.inc(sy.dma_start(
                            out=(xc0 if p % 2 == 0 else xc1)[:, 0:512],
                            in_=dr_slab(mix_dr, T, p)[:, tt * 512:(tt + 1) * 512]), 16)
                        dls.inc(sy.dma_start(
                            out=(xc0 if p % 2 == 0 else xc1)[:, 512:1024],
                            in_=xchunk_ap(p)[:, tt * 512:(tt + 1) * 512]), 16)

            @b.tensor
            def _(t):
                t.wait_ge(dl.h, dl_t[('pl', KC - 1)])
                for tt in range(4):
                    for m in range(NMID):
                        t.wait_ge(dl.h, dl_t[('wfc', tt, m)])
                        if tt > 0 and m < 2:
                            t.wait_ge(vcS.h, vc_t[('g', tt - 1, 7)])
                        if m >= 2:
                            t.wait_ge(vcS.h, vc_t[('mid', tt, m - 2)])
                        slot = (m % 2) * 512
                        ins = None
                        for k in range(KC):
                            rh = planeA[:, k * T + tt * 512: k * T + tt * 512 + 512]
                            ins = t.matmul(psA[:, slot:slot + 512],
                                           wsl[m % 2][:, k * 128:(k + 1) * 128], rh,
                                           start=(k == 0), stop=(k == KC - 1),
                                           skip_group_check=True)
                        mmS.inc(ins, 1)
                    t.wait_ge(vcS.h, vc_t[('mid', tt, NMID - 1)])
                    for m2 in range(NMID):
                        t.wait_ge(dl.h, dl_t[('wpj', tt, m2)])
                        ins = None
                        for mg in range(4):
                            for i in range(2):
                                mo = 2 * mg + i
                                ins = t.matmul(PS[mg][:, i * 512:(i + 1) * 512],
                                               wsl[m2 % 4][:, mo * 128:(mo + 1) * 128],
                                               midb[:, m2 * 512:(m2 + 1) * 512],
                                               start=(m2 == 0), stop=(m2 == NMID - 1),
                                               skip_group_check=True)
                        mmS.inc(ins, 1)

            @b.scalar
            def _(s):
                for tt in range(4):
                    for m in range(NMID):
                        s.wait_ge(mmS.h, mm_t[('z', tt, m)])
                        if m >= 1:
                            s.wait_ge(vcS.h, vc_t[('mid', tt, m - 1)])
                        elif tt > 0:
                            s.wait_ge(vcS.h, vc_t[('mid', tt - 1, NMID - 1)])
                        slot = (m % 2) * 512
                        evS.inc(s.activation(rlb, psA[:, slot:slot + 512], AF.Relu), 1)
                    for p in range(8):
                        s.wait_ge(gpS.h, gp_t[('add', tt, p, 3)])
                        pk = prev_key(None, tt, p, 1)
                        if pk is not None:
                            s.wait_ge(vcS.h, vc_t[('g', pk[0], pk[1])])
                        evS.inc(s.activation(gpc, sacc, AF.Silu, scale=0.2), 1)

            @b.vector
            def _(v):
                for tt in range(4):
                    for m in range(NMID):
                        v.wait_ge(evS.h, ev_t[('relu', tt, m)])
                        slot = (m % 2) * 512
                        ins = v.tensor_mul(midb[:, m * 512:(m + 1) * 512], rlb,
                                           psA[:, slot:slot + 512])
                        vcS.inc(ins, 1)
                    v.wait_ge(mmS.h, mm_t[('ff', tt, NMID - 1)])
                    for p in range(8):
                        mg, i = p // 2, p % 2
                        pk = prev_key(None, tt, p, 2)
                        if pk is not None:
                            v.wait_ge(dl.h, dl_t[('st', pk[0], pk[1])])
                        if tt * 8 + p >= 1:
                            pk1 = prev_key(None, tt, p, 1)
                            v.wait_ge(gpS.h, gp_t[('add', pk1[0], pk1[1], 3)])
                        vcS.inc(v.tensor_copy(ypc, PS[mg][:, i * 512:(i + 1) * 512]), 1)
                        v.tensor_scalar(spkA, ypc, 1.0, None, ALU.is_gt)
                        if tt * 8 + p >= 1:
                            pk1 = prev_key(None, tt, p, 1)
                            v.wait_ge(evS.h, ev_t[('silu', pk1[0], pk1[1])])
                        v.tensor_copy(smem, ypc)
                        vcS.inc(v.tensor_copy(sacc, spkA), 1)
                        for it in range(4):
                            if it > 0:
                                v.wait_ge(gpS.h, gp_t[('add', tt, p, it - 1)])
                            ins = v.scalar_tensor_tensor(smem, smem, 0.9, ypc,
                                                         ALU.mult, ALU.add)
                            vcS.inc(ins, 1)
                            v.wait_ge(gpS.h, gp_t[('sub', tt, p, it)])
                            curb = spkB if it % 2 == 0 else spkA
                            ins = v.tensor_scalar(curb[:, :], smem, 1.0, None, ALU.is_gt)
                            vcS.inc(ins, 1)
                        v.wait_ge(evS.h, ev_t[('silu', tt, p)])
                        v.wait_ge(dls.h, dls_t[('sp', tt, p)])
                        v.tensor_mul(gpc, gpc,
                                     (xc0 if p % 2 == 0 else xc1)[:, 0:512])
                        v.wait_ge(dls.h, dls_t[('xp', tt, p)])
                        ins = v.tensor_add(ev0[:, (p % 2) * 512:(p % 2) * 512 + 512],
                                           gpc, (xc0 if p % 2 == 0 else xc1)[:, 512:1024])
                        vcS.inc(ins, 1)

        with nc.Block() as b:
            g0, m0, e0 = dl.base(), mmS.base(), evS.base()

            @b.gpsimd
            def _(g):
                for c in range(KC):
                    if c >= 2:
                        g.wait_ge(mmS.h, m0 + (c - 1))
                    dl.inc(g.dma_start(out=(cbf0 if c % 2 == 0 else cbf1)[:, :],
                                       in_=dr_slab(x1f, T, c)), 16)

            @b.scalar
            def _(s):
                for c in range(KC):
                    s.wait_ge(dl.h, g0 + 16 * (c + 1))
                    if c >= 2:
                        s.wait_ge(mmS.h, m0 + (c - 1))
                    src = cbf0 if c % 2 == 0 else cbf1
                    dst = rsl0 if c % 2 == 0 else rsl1
                    evS.inc(s.activation(dst[:, 0:T], src[:, :], AF.Square), 1)

            @b.tensor
            def _(t):
                for c in range(KC):
                    t.wait_ge(evS.h, e0 + (c + 1))
                    xb = cbf0 if c % 2 == 0 else cbf1
                    sq = (rsl0 if c % 2 == 0 else rsl1)[:, 0:T]
                    for nt in range(4):
                        pt = [psA, psA, psB, psB][nt]
                        t.matmul(pt[0:1, (nt % 2) * 512:(nt % 2) * 512 + 512],
                                 onesc[:, :], xb[:, nt * 512:(nt + 1) * 512],
                                 start=(c == 0), stop=(c == KC - 1))
                    ins = None
                    for nt in range(4):
                        pt = [psC, psC, psD, psD][nt]
                        ins = t.matmul(pt[0:1, (nt % 2) * 512:(nt % 2) * 512 + 512],
                                       onesc[:, :], sq[:, nt * 512:(nt + 1) * 512],
                                       start=(c == 0), stop=(c == KC - 1))
                    mmS.inc(ins, 1)

        with nc.Block() as b:
            v0, e0 = vcS.base(), evS.base()

            @b.vector
            def _(v):
                v.tensor_scalar(R1[0:1, 0:1024], psA[0:1, :], 1.0 / D, None, ALU.mult)
                v.tensor_scalar(R1[0:1, 1024:2048], psB[0:1, :], 1.0 / D, None, ALU.mult)
                v.tensor_scalar(R2[0:1, 0:1024], psC[0:1, :], 1.0 / D, None, ALU.mult)
                v.tensor_scalar(R2[0:1, 1024:2048], psD[0:1, :], 1.0 / D, None, ALU.mult)
                v.scalar_tensor_tensor(R3[0:1, :], R1[0:1, :], 1.0, R1[0:1, :], ALU.mult, ALU.mult)
                v.tensor_sub(R3[0:1, :], R2[0:1, :], R3[0:1, :])
                ins = v.tensor_scalar(R3[0:1, :], R3[0:1, :], 1e-5, None, ALU.add)
                vcS.inc(ins, 1)
                v.wait_ge(evS.h, e0 + 1)
                ins = v.tensor_mul(R1[0:1, :], R1[0:1, :], R4[0:1, :])
                vcS.inc(ins, 1)

            @b.scalar
            def _(s):
                s.wait_ge(vcS.h, v0 + 1)
                s.activation(R4[0:1, :], R3[0:1, :], AF.Ln)
                evS.inc(s.activation(R4[0:1, :], R4[0:1, :], AF.Exp, scale=-0.5), 1)

        with nc.Block() as b:
            m0 = mmS.base()

            @b.tensor
            def _(t):
                for half in range(2):
                    for rr, dst in [(R4, psA), (R1, psB)]:
                        ins = None
                        for nt in range(2):
                            off = half * 1024 + nt * 512
                            ins = t.matmul(dst[:, nt * 512:(nt + 1) * 512], ones1[0:1, :],
                                           rr[0:1, off:off + 512], start=True, stop=True)
                        mmS.inc(ins, 1)
                    if half == 0:
                        t.wait_ge(vcS.h, vcS.v + 1)

            @b.vector
            def _(v):
                for half in range(2):
                    v.wait_ge(mmS.h, m0 + 2 * (half + 1))
                    for src, dst in [(psA, rbc), (psB, mrbc)]:
                        ins = v.tensor_copy(dst[:, half * 1024:half * 1024 + 1024], src[:, :])
                    vcS.inc(ins, 1)

        with nc.Block() as b:
            s0, v0, g0 = dls.base(), vcS.base(), dl.base()

            @b.sync
            def _(sy):
                for c in range(KC):
                    if c >= 2:
                        sy.wait_ge(vcS.h, v0 + (c - 1))
                    dls.inc(sy.dma_start(out=(xc0 if c % 2 == 0 else xc1)[:, :],
                                         in_=dr_slab(x1f, T, c)), 16)

            @b.vector
            def _(v):
                for c in range(KC):
                    v.wait_ge(dls.h, s0 + 16 * (c + 1))
                    if c >= 2:
                        v.wait_ge(dl.h, g0 + 16 * (c - 1))
                    xc = xc0 if c % 2 == 0 else xc1
                    v.tensor_mul(TA[:, 0:T], xc[:, :], rbc[:, :])
                    ins = v.tensor_sub(planeA[:, c * T:(c + 1) * T], TA[:, 0:T], mrbc[:, :])
                    vcS.inc(ins, 1)

            @b.gpsimd
            def _(g):
                for c in range(KC):
                    g.wait_ge(vcS.h, v0 + c + 1)
                    dl.inc(g.dma_start(out=dr_slab(xn1f, T, c),
                                       in_=planeA[:, c * T:(c + 1) * T]), 16)
                g.wait_ge(dl.h, dl.v)

        with nc.Block() as b:
            g0, m0, e0 = dl.base(), mmS.base(), evS.base()
            wl_done, st_done = {}, {}
            dv = g0
            for m in range(KC):
                dv += 16
                wl_done[m] = dv
                if m >= 1:
                    dv += 16
                    st_done[m - 1] = dv
            dv += 16
            st_done[KC - 1] = dv

            @b.gpsimd
            def _(g):
                for m in range(KC):
                    if m >= 2:
                        g.wait_ge(mmS.h, m0 + 2 * KC * (m - 1))
                    dl.inc(g.dma_start(out=wsl[m % 2][:, :],
                                       in_=stash_slab_ap(w1h_s, 8, 8, m)), 16)
                    if m >= 1:
                        g.wait_ge(evS.h, e0 + m)
                        dl.inc(g.dma_start(out=dr_slab(basef, T, m - 1),
                                           in_=(cbf0 if (m - 1) % 2 == 0 else cbf1)[:, :]), 16)
                g.wait_ge(evS.h, e0 + KC)
                dl.inc(g.dma_start(out=dr_slab(basef, T, KC - 1),
                                   in_=(cbf0 if (KC - 1) % 2 == 0 else cbf1)[:, :]), 16)
                g.wait_ge(dl.h, dl.v)

            @b.tensor
            def _(t):
                for m in range(KC):
                    t.wait_ge(dl.h, wl_done[m])
                    if m >= 2:
                        t.wait_ge(evS.h, e0 + m - 1)
                    ins = None
                    for k in range(KC):
                        for half in range(2):
                            for nt in range(2):
                                n0 = half * 1024 + nt * 512
                                ins = t.matmul((psA, psB)[half][:, nt * 512:(nt + 1) * 512]
                                               if m % 2 == 0 else
                                               (psC, psD)[half][:, nt * 512:(nt + 1) * 512],
                                               wsl[m % 2][:, k * 128:(k + 1) * 128],
                                               planeA[:, k * T + n0:k * T + n0 + 512],
                                               start=(k == 0), stop=(k == KC - 1),
                                               skip_group_check=True)
                    mmS.inc(ins, 2 * KC)

            @b.vector
            def _(v):
                for m in range(KC):
                    v.wait_ge(mmS.h, m0 + 2 * KC * (m + 1))
                    if m >= 2:
                        v.wait_ge(dl.h, st_done[m - 2])
                    pair = (psA, psB) if m % 2 == 0 else (psC, psD)
                    dst = cbf0 if m % 2 == 0 else cbf1
                    ins = None
                    for half in range(2):
                        ins = v.tensor_scalar(dst[:, half * 1024:half * 1024 + 1024],
                                              pair[half][:, :], b1s[:, m:m + 1], None, ALU.add)
                    evS.inc(ins, 1)

        if debug_outs:
            dbg = {}
            for nm, src, shape, dt in [("d_h", h_dr, [128, KC * TH], BF16),
                                       ("d_loc", loc_dr, [128, KC * T], BF16),
                                       ("d_st", st_dr, [128, KC * T], BF16),
                                       ("d_mem", mem_dr, [128, KC * T], BF16),
                                       ("d_mix", mix_dr, [128, KC * T], F32),
                                       ("d_xn", xn_dr, [128, KC * T], BF16)]:
                dbg[nm] = nc.dram_tensor(nm, shape, dt, kind="ExternalOutput")
            with nc.Block() as b:
                @b.gpsimd
                def _(g):
                    for nm, src in [("d_h", h_dr), ("d_loc", loc_dr), ("d_st", st_dr),
                                    ("d_mem", mem_dr), ("d_mix", mix_dr), ("d_xn", xn_dr)]:
                        dl.inc(g.dma_start(out=dbg[nm].ap(), in_=src.ap()), 16)
                    g.wait_ge(dl.h, dl.v)
        es.close()
        return nc

    return build_A

build_A = _mk_A()

def _mk_B():

    F32 = mybir.dt.float32
    BF16 = mybir.dt.bfloat16
    AF = mybir.ActivationFunctionType
    ALU = mybir.AluOpType
    AX = mybir.AxisListType

    T8 = 8192
    ST = 1024
    NST = T8 // ST
    CH = 128
    NCHS = ST // CH
    HB = 63


    def build_B(debug_outs=False):
        nc = bass.Bass("TRN2", target_bir_lowering=False)

        def din(name, shape, dt=F32):
            return nc.dram_tensor(name, shape, dt, kind="ExternalInput")

        xnf = din("xnf", [128, 8 * T8], BF16)
        bsf = din("bsf", [128, 8 * T8], BF16)
        wq = din("wq", [1024, 256]); wk = din("wk", [1024, 256]); wv = din("wv", [1024, 256])
        wbv = din("wbv", [128, 8])
        cwq = din("cwq", [128, 8]); cwk = din("cwk", [128, 8]); cwv = din("cwv", [128, 8])
        fsh = din("fsh", [128, 10])
        flg = din("flg", [128, 128])
        w1sd = din("w1sd", [16, 1024])
        w2d = din("w2d", [1024, 4])
        b2d = din("b2d", [1, 4])
        ltd = din("ltd", [128, 1])
        onrm = nc.dram_tensor("onrm", [T8, 256], BF16, kind="ExternalOutput")

        es = ExitStack()

        def sb(name, shape, dt):
            return es.enter_context(nc.sbuf_tensor(name, shape, dt))

        wqs = sb("wqs", [128, 8 * 256], BF16)
        wks = sb("wks", [128, 8 * 256], BF16)
        wvs = sb("wvs", [128, 8 * 256], BF16)
        wbs = sb("wbs", [128, 8], BF16)
        w1ss = sb("w1ss", [16, 1024], BF16)
        w2s = sb("w2s", [128, 8 * 4], BF16)
        b2s = sb("b2s", [1, 4], BF16)
        cwqs = sb("cwqs", [128, 8], F32); cwks = sb("cwks", [128, 8], F32)
        cwvs = sb("cwvs", [128, 8], F32)
        fshs = sb("fshs", [128, 10], F32); flgs = sb("flgs", [128, 128], F32)
        invT = sb("invT", [128, 1], F32)
        identf = sb("identf", [128, 128], F32)
        identb = sb("identb", [128, 128], BF16)
        iotp = sb("iotp", [128, 1], F32)
        onesb = sb("onesb", [1, 128], BF16)
        onesc = sb("onesc", [128, 1], BF16)
        maskNS = sb("maskNS", [128, 128], F32)
        maskUI = sb("maskUI", [128, 128], F32)
        dq = sb("dq", [128, 8 * 128], BF16)
        dkv = sb("dkv", [128, 8 * 128], BF16)
        dvv = sb("dvv", [128, 8 * 128], BF16)
        dfs = sb("dfs", [128, 10 * 128], BF16)
        dfl = sb("dfl", [128, 64 * 128], BF16)
        xnsl = sb("xnsl", [128, 8 * (ST + 3)], BF16)
        bsl8 = sb("bsl8", [128, 8 * ST], BF16)
        qraw = sb("qraw", [128, 2 * (ST + 3)], BF16)
        kraw = sb("kraw", [128, 2 * (ST + 3)], BF16)
        vraw = sb("vraw", [128, 2 * (ST + 3)], BF16)
        qs_ = sb("qs_", [128, 2 * ST], BF16)
        ks_ = sb("ks_", [128, 2 * ST], BF16)
        vext = sb("vext", [128, 2 * (HB + ST)], BF16)
        qn = sb("qn", [128, 2 * ST], BF16)
        kn = sb("kn", [128, 2 * ST], BF16)
        kb = sb("kb", [128, 2 * ST], BF16)
        vb = sb("vb", [128, 2 * ST], BF16)
        kn_tm = sb("kn_tm", [128, 8 * 256], BF16)
        kb_tm = sb("kb_tm", [128, 8 * 256], BF16)
        vb_tm = sb("vb_tm", [128, 8 * 256], BF16)
        v_tm = sb("v_tm", [128, 8 * 256], BF16)
        dl_tm = sb("dl_tm", [128, 8 * 256], BF16)
        ls_fm = sb("ls_fm", [128, 2 * ST], BF16)
        ll_fm = sb("ll_fm", [128, 2 * ST], BF16)
        ls_tm = sb("ls_tm", [128, 8 * 256], BF16)
        ll_tm = sb("ll_tm", [128, 8 * 256], BF16)
        h1f = sb("h1f", [128, 8 * ST], BF16)
        u0s = sb("u0s", [128, 8 * 256], BF16)
        wws = sb("wws", [128, 8 * 256], BF16)
        wTs = sb("wTs", [128, 8 * 256], BF16)
        alTs = sb("alTs", [128, 8 * 128], BF16)
        Abf = sb("Abf", [128, 128], BF16)
        Bbf = sb("Bbf", [128, 128], BF16)
        Ap2 = sb("Ap2", [128, 128], BF16)
        Bp2 = sb("Bp2", [128, 128], BF16)
        Nn0 = sb("Nn0", [128, 128], BF16)
        Nn1 = sb("Nn1", [128, 128], BF16)
        S_sb = sb("S_sb", [128, 512], BF16)
        u_bf = sb("u_bf", [128, 256], BF16)
        rq = sb("rq", [1, ST], F32)
        rk = sb("rk", [1, ST], F32)
        rkb = sb("rkb", [1, ST], F32)
        brow = sb("brow", [1, ST], F32)
        sqs = sb("sqs", [128, ST], BF16)
        sums = sb("sums", [128, 12 * 8], F32)
        st_all = sb("st_all", [128, 8 * 16], F32)
        st_bf = sb("st_bf", [128, 16], BF16)
        stf = sb("stf", [16, ST], BF16)
        logit = sb("logit", [128, 8 * 4], F32)
        zb32 = sb("zb32", [128, 32], F32)
        eb32 = sb("eb32", [128, 32], F32)
        mx8 = sb("mx8", [128, 8], F32)
        nmx8 = sb("nmx8", [128, 8], F32)
        omx = qraw[:, 0:2048]
        ot_b8 = omx
        oacc = sb("oacc", [128, 256], F32)
        oacc2 = sb("oacc2", [128, 256], F32)
        scr = sqs.bitcast(F32)[:, 0:512]

        S_ps = es.enter_context(nc.psum_tensor("S_ps", [128, 512], F32))
        psU = [es.enter_context(nc.psum_tensor(f"psU{i}", [128, 512], F32)) for i in range(2)]
        psO = [es.enter_context(nc.psum_tensor(f"psO{i}", [128, 512], F32)) for i in range(2)]
        psP = [es.enter_context(nc.psum_tensor(f"psP{i}", [128, 512], F32)) for i in range(2)]

        class Sm:
            def __init__(self, name):
                self.h = es.enter_context(nc.semaphore(name))
                self.v = 0

            def inc(self, instr, n=1):
                instr.then_inc(self.h, n)
                self.v += n

        dl = Sm("dl"); dls = Sm("dls"); mmS = Sm("mmS"); evS = Sm("evS"); vcS = Sm("vcS")

        def xn_slab_ap(t0, w):
            return xnf.ap().rearrange("p (k t) -> p k t", k=8)[:, :, t0:t0 + w]

        def bs_slab_ap(t0, w):
            return bsf.ap().rearrange("p (k t) -> p k t", k=8)[:, :, t0:t0 + w]

        with nc.Block() as b:
            @b.gpsimd
            def _(g):
                for dst, src in [(cwqs, cwq), (cwks, cwk), (cwvs, cwv), (fshs, fsh),
                                 (flgs, flg), (invT, ltd), (wbs, wbv), (w1ss, w1sd),
                                 (b2s, b2d)]:
                    dl.inc(g.dma_start(out=dst[:, :], in_=src.ap()), 16)
                for dst, src in [(wqs, wq), (wks, wk), (wvs, wv)]:
                    dl.inc(g.dma_start(out=dst[:, :].rearrange("p (k m) -> p k m", k=8),
                                       in_=src.ap().rearrange("(k p) m -> p k m", p=128)), 16)
                dl.inc(g.dma_start(out=w2s[:, :].rearrange("p (k m) -> p k m", k=8),
                                   in_=w2d.ap().rearrange("(k p) m -> p k m", p=128)), 16)
                g.iota(identf[:, :], [[1, 128]], channel_multiplier=0,
                       allow_small_or_imprecise_dtypes=True)
                g.iota(iotp[:, :1], [[1, 1]], channel_multiplier=1,
                       allow_small_or_imprecise_dtypes=True)
                g.memset(onesb[:, :], 1.0)
                g.memset(onesc[:, :], 1.0)
                g.memset(vext[:, :], 0.0)
                g.wait_ge(dl.h, dl.v)

        with nc.Block() as b:
            @b.vector
            def _(v):
                v.tensor_scalar(maskNS[:, :], identf[:, :], iotp[:, :1], None, ALU.is_lt)
                v.tensor_scalar(maskNS[:, :], maskNS[:, :], -1.0, None, ALU.mult)
                v.tensor_scalar(maskUI[:, :], identf[:, :], iotp[:, :1], None, ALU.is_ge)
                v.tensor_scalar(identf[:, :], identf[:, :], iotp[:, :1], None, ALU.is_equal)
                v.tensor_copy(identb[:, :], identf[:, :])
                for dg, cw in [(dq, cwqs), (dkv, cwks), (dvv, cwvs)]:
                    for j in range(8):
                        v.tensor_scalar(dg[:, j * 128:(j + 1) * 128], identf[:, :],
                                        cw[:, j:j + 1], None, ALU.mult)
                for j in range(10):
                    v.tensor_scalar(dfs[:, j * 128:(j + 1) * 128], identf[:, :],
                                    fshs[:, j:j + 1], None, ALU.mult)

            @b.scalar
            def _(s):
                s.activation(invT[:, :], invT[:, :], AF.Exp, scale=-1.0)

        onesf = sb("onesf", [1, 128], F32)
        with nc.Block() as b:
            @b.gpsimd
            def _(g):
                g.memset(onesf[:, :], 1.0)

        for s in range(NST):
            S0 = s * ST
            W = ST + 3 if s > 0 else ST
            t0 = S0 - 3 if s > 0 else S0
            off = 3 if s == 0 else 0

            with nc.Block() as b:
                d0, m0, v0 = dl.v, mmS.v, vcS.v

                @b.gpsimd
                def _(g, s=s, t0=t0, W=W, S0=S0):
                    dl.inc(g.dma_start(
                        out=xnsl[:, :].rearrange("p (k t) -> p k t", k=8)[:, :, 0:W],
                        in_=xn_slab_ap(t0, W)), 16)
                    dl.inc(g.dma_start(out=bsl8[:, :].rearrange("p (k t) -> p k t", k=8),
                                       in_=bs_slab_ap(S0, ST)), 16)
                    if s == 0:
                        g.memset(qraw[:, 0:3], 0.0)
                        g.memset(qraw[:, ST + 3:ST + 6], 0.0)
                        g.memset(kraw[:, 0:3], 0.0)
                        g.memset(kraw[:, ST + 3:ST + 6], 0.0)
                        g.memset(vraw[:, 0:3], 0.0)
                        g.memset(vraw[:, ST + 3:ST + 6], 0.0)
                    g.wait_ge(dl.h, dl.v)

                @b.tensor
                def _(t, W=W, off=off, d0=d0):
                    t.wait_ge(dl.h, d0 + 32)
                    u = 0
                    for ti, wsl_ in enumerate([wqs, wks, wvs]):
                        for mt in range(2):
                            nts = [(o_, w_) for (o_, w_) in
                                   [(0, 512), (512, 512), (1024, W - 1024)] if w_ > 0]
                            for (no, nw) in nts:
                                pp = psP[u % 2]
                                if u >= 2:
                                    t.wait_ge(vcS.h, v0 + u - 1)
                                ins = None
                                for k in range(8):
                                    ins = t.matmul(
                                        pp[:, 0:nw],
                                        wsl_[:, k * 256 + mt * 128:k * 256 + mt * 128 + 128],
                                        xnsl[:, k * (ST + 3) + no:k * (ST + 3) + no + nw],
                                        start=(k == 0), stop=(k == 7), skip_group_check=True)
                                mmS.inc(ins, 1)
                                u += 1
                    ins = None
                    for nt in range(2):
                        for k in range(8):
                            ins = t.matmul(psU[nt][0:1, 0:512], wbs[:, k:k + 1],
                                           xnsl[:, k * (ST + 3) + (W - ST) + nt * 512:
                                                k * (ST + 3) + (W - ST) + nt * 512 + 512],
                                           start=(k == 0), stop=(k == 7), skip_group_check=True)
                    mmS.inc(ins, 1)

                @b.vector
                def _(v, W=W, off=off):
                    u = 0
                    nu = 0
                    for ti, dst in enumerate([qraw, kraw, vraw]):
                        for mt in range(2):
                            nts = [(o_, w_) for (o_, w_) in
                                   [(0, 512), (512, 512), (1024, W - 1024)] if w_ > 0]
                            for (no, nw) in nts:
                                v.wait_ge(mmS.h, m0 + u + 1)
                                ins = v.tensor_copy(
                                    dst[:, mt * (ST + 3) + off + no:
                                        mt * (ST + 3) + off + no + nw],
                                    psP[u % 2][:, 0:nw])
                                vcS.inc(ins, 1)
                                u += 1
                    v.wait_ge(mmS.h, m0 + u + 1)
                    v.tensor_copy(brow[0:1, 0:512], psU[0][0:1, 0:512])
                    ins = v.tensor_copy(brow[0:1, 512:1024], psU[1][0:1, 0:512])
                    vcS.inc(ins, 1)

                @b.scalar
                def _(s_, v0=v0):
                    s_.wait_ge(vcS.h, vcS.v)
                    evS.inc(s_.activation(brow[0:1, :], brow[0:1, :], AF.Sigmoid), 1)

            with nc.Block() as b:
                m0, e0 = mmS.v, evS.v

                @b.tensor
                def _(t):
                    u = 0
                    for (raw, dg) in [(qraw, dq), (kraw, dkv), (vraw, dvv)]:
                        for c2 in range(2):
                            for nt in range(2):
                                pp = psP[u % 2]
                                if u >= 2:
                                    t.wait_ge(evS.h, e0 + u - 1)
                                ins = None
                                for tap in range(4):
                                    ins = t.matmul(
                                        pp[:, 0:512],
                                        dg[:, (c2 * 4 + tap) * 128:(c2 * 4 + tap) * 128 + 128],
                                        raw[:, c2 * (ST + 3) + tap + nt * 512:
                                            c2 * (ST + 3) + tap + nt * 512 + 512],
                                        start=(tap == 0), stop=(tap == 3),
                                        skip_group_check=True)
                                mmS.inc(ins, 1)
                                u += 1

                @b.scalar
                def _(s_):
                    u = 0
                    for (raw, dst) in [(qraw, qs_), (kraw, ks_), (vraw, None)]:
                        for c2 in range(2):
                            for nt in range(2):
                                s_.wait_ge(mmS.h, m0 + u + 1)
                                if dst is None:
                                    o = vext[:, c2 * (HB + ST) + HB + nt * 512:
                                             c2 * (HB + ST) + HB + nt * 512 + 512]
                                else:
                                    o = dst[:, c2 * ST + nt * 512:c2 * ST + nt * 512 + 512]
                                evS.inc(s_.activation(o, psP[u % 2][:, 0:512], AF.Silu), 1)
                                u += 1

            with nc.Block() as b:
                m0, e0, v0 = mmS.v, evS.v, vcS.v

                @b.scalar
                def _(s_):
                    uu = 0
                    for i, src in enumerate([qs_, ks_]):
                        for c2 in range(2):
                            if uu > 0:
                                s_.wait_ge(mmS.h, m0 + uu)
                            evS.inc(s_.activation(sqs[:, :], src[:, c2 * ST:(c2 + 1) * ST],
                                                  AF.Square), 1)
                            uu += 1
                    s_.wait_ge(vcS.h, v0 + 1)
                    s_.activation(rq[0:1, :], rq[0:1, :], AF.Ln)
                    s_.activation(rq[0:1, :], rq[0:1, :], AF.Exp, scale=-0.5)
                    s_.activation(rk[0:1, :], rk[0:1, :], AF.Ln)
                    ins = s_.activation(rk[0:1, :], rk[0:1, :], AF.Exp, scale=-0.5)
                    evS.inc(ins, 1)

                @b.tensor
                def _(t):
                    uu = 0
                    for i in range(2):
                        for c2 in range(2):
                            t.wait_ge(evS.h, e0 + uu + 1)
                            ins = None
                            for nt in range(2):
                                dst = (psP[nt] if i == 0 else psU[nt])
                                ins = t.matmul(dst[0:1, 0:512],
                                               onesc[:, :], sqs[:, nt * 512:nt * 512 + 512],
                                               start=(c2 == 0), stop=(c2 == 1),
                                               skip_group_check=True)
                            mmS.inc(ins, 1)
                            uu += 1
                    t.wait_ge(evS.h, e0 + 5)
                    t.wait_ge(vcS.h, v0 + 2)
                    for r, rr in enumerate([rq, rk, rkb, brow]):
                        if r >= 2:
                            t.wait_ge(vcS.h, v0 + 2 + (r - 1))
                        ins = t.matmul(psU[r % 2][:, 0:512], onesf[0:1, :],
                                       rr[0:1, 0:512], start=True, stop=True)
                        ins = t.matmul(psO[r % 2][:, 0:512], onesf[0:1, :],
                                       rr[0:1, 512:1024], start=True, stop=True)
                        mmS.inc(ins, 1)

                @b.vector
                def _(v):
                    v.wait_ge(mmS.h, m0 + 4)
                    v.tensor_scalar(rq[0:1, 0:512], psP[0][0:1, 0:512], 1e-6, None, ALU.add)
                    v.tensor_scalar(rq[0:1, 512:1024], psP[1][0:1, 0:512], 1e-6, None, ALU.add)
                    v.tensor_scalar(rk[0:1, 0:512], psU[0][0:1, 0:512], 1e-6, None, ALU.add)
                    ins = v.tensor_scalar(rk[0:1, 512:1024], psU[1][0:1, 0:512], 1e-6, None, ALU.add)
                    vcS.inc(ins, 1)
                    v.wait_ge(evS.h, e0 + 5)
                    ins = v.tensor_mul(rkb[0:1, :], rk[0:1, :], brow[0:1, :])
                    vcS.inc(ins, 1)
                    plist = [(qs_, qn), (ks_, kn), (ks_, kb), (None, vb)]
                    for r, (srcp, dstp) in enumerate(plist):
                        v.wait_ge(mmS.h, m0 + 4 + r + 1)
                        for c2 in range(2):
                            src = (srcp[:, c2 * ST:(c2 + 1) * ST] if srcp is not None else
                                   vext[:, c2 * (HB + ST) + HB:c2 * (HB + ST) + HB + ST])
                            v.tensor_mul(dstp[:, c2 * ST:c2 * ST + 512],
                                         src[:, 0:512], psU[r % 2][:, 0:512])
                            ins = v.tensor_mul(dstp[:, c2 * ST + 512:c2 * ST + 1024],
                                               src[:, 512:1024], psO[r % 2][:, 0:512])
                        vcS.inc(ins, 1)

            with nc.Block() as b:
                m0, v0 = mmS.v, vcS.v

                @b.tensor
                def _(t):
                    u = 0
                    for (srcp, dstp) in [(kn, kn_tm), (kb, kb_tm), (vb, vb_tm), (None, v_tm)]:
                        for c2 in range(2):
                            for tt in range(8):
                                if u >= 2:
                                    t.wait_ge(vcS.h, v0 + u - 1)
                                src = (srcp[:, c2 * ST + tt * 128:c2 * ST + tt * 128 + 128]
                                       if srcp is not None else
                                       vext[:, c2 * (HB + ST) + HB + tt * 128:
                                            c2 * (HB + ST) + HB + tt * 128 + 128])
                                ins = t.transpose(psO[u % 2].bitcast(BF16)[:, 0:128],
                                                  src, identb[:, :])
                                mmS.inc(ins, 1)
                                u += 1

                @b.vector
                def _(v):
                    u = 0
                    for (srcp, dstp) in [(kn, kn_tm), (kb, kb_tm), (vb, vb_tm), (None, v_tm)]:
                        for c2 in range(2):
                            for tt in range(8):
                                v.wait_ge(mmS.h, m0 + u + 1)
                                ins = v.tensor_copy(
                                    dstp[:, tt * 256 + c2 * 128:tt * 256 + c2 * 128 + 128],
                                    psO[u % 2].bitcast(BF16)[:, 0:128])
                                vcS.inc(ins, 1)
                                u += 1

            A8 = kraw[:, 0:1024]
            B8 = kraw[:, 1024:2048]
            A8b = vraw[:, 0:1024]
            B8b = vraw[:, 1024:2048]
            N8a = qs_[:, 0:1024]
            N8b = qs_[:, 1024:2048]

            with nc.Block() as b:
                m0, v0 = mmS.v, vcS.v

                @b.tensor
                def _(t):
                    u = 0
                    for kind in ("G", "alT"):
                        for ch in range(NCHS):
                            if u >= 2:
                                t.wait_ge(vcS.h, v0 + u - 1)
                            pp = psP[u % 2]
                            ins = None
                            for c2 in range(2):
                                lhs = (kb if kind == "G" else kn)[:, c2 * ST + ch * 128:
                                                                  c2 * ST + ch * 128 + 128]
                                rhs = (kn if kind == "G" else qn)[:, c2 * ST + ch * 128:
                                                                  c2 * ST + ch * 128 + 128]
                                ins = t.matmul(pp[:, 0:128], lhs, rhs, start=(c2 == 0),
                                               stop=(c2 == 1), skip_group_check=True)
                            mmS.inc(ins, 1)
                            u += 1

                @b.vector
                def _(v):
                    u = 0
                    for kind in ("G", "alT"):
                        for ch in range(NCHS):
                            v.wait_ge(mmS.h, m0 + u + 1)
                            if kind == "G":
                                ins = v.tensor_mul(A8[:, ch * 128:ch * 128 + 128],
                                                   psP[u % 2][:, 0:128], maskNS[:, :])
                            else:
                                ins = v.tensor_mul(alTs[:, ch * 128:ch * 128 + 128],
                                                   psP[u % 2][:, 0:128], maskUI[:, :])
                            vcS.inc(ins, 1)
                            u += 1

            with nc.Block() as b:
                m0, v0 = mmS.v, vcS.v

                @b.tensor
                def _(t):
                    for ch in range(NCHS):
                        if ch >= 2:
                            t.wait_ge(vcS.h, v0 + ch - 1)
                        ins = t.transpose(psO[ch % 2].bitcast(BF16)[:, 0:128],
                                          A8[:, ch * 128:ch * 128 + 128], identb[:, :])
                        mmS.inc(ins, 1)

                @b.vector
                def _(v):
                    for ch in range(NCHS):
                        v.wait_ge(mmS.h, m0 + ch + 1)
                        ins = v.tensor_copy(B8[:, ch * 128:ch * 128 + 128],
                                            psO[ch % 2].bitcast(BF16)[:, 0:128])
                        vcS.inc(ins, 1)

            with nc.Block() as b:
                @b.vector
                def _(v):
                    for ch in range(NCHS):
                        v.tensor_add(N8a[:, ch * 128:ch * 128 + 128], identb[:, :],
                                     B8[:, ch * 128:ch * 128 + 128])

            Acur, Anxt = A8, A8b
            Bcur, Bnxt = B8, B8b
            Ncur, Nnxt = N8a, N8b
            for kk in range(1, 7):
                with nc.Block() as b:
                    m0, v0 = mmS.v, vcS.v

                    @b.tensor
                    def _(t, Acur=Acur, Bcur=Bcur, Anxt=Anxt, Bnxt=Bnxt):
                        u = 0
                        for ch in range(NCHS):
                            for which in range(2):
                                if u >= 2:
                                    t.wait_ge(vcS.h, v0 + u - 1)
                                pp = psP[u % 2]
                                lhs = (Bcur if which == 0 else Acur)[:, ch * 128:ch * 128 + 128]
                                rhs = (Acur if which == 0 else Bcur)[:, ch * 128:ch * 128 + 128]
                                ins = t.matmul(pp[:, 0:128], lhs, rhs, start=True, stop=True,
                                               skip_group_check=True)
                                mmS.inc(ins, 1)
                                u += 1

                    @b.vector
                    def _(v, Anxt=Anxt, Bnxt=Bnxt):
                        u = 0
                        for ch in range(NCHS):
                            for which in range(2):
                                v.wait_ge(mmS.h, m0 + u + 1)
                                dst = (Anxt if which == 0 else Bnxt)
                                ins = v.tensor_copy(dst[:, ch * 128:ch * 128 + 128],
                                                    psP[u % 2][:, 0:128])
                                vcS.inc(ins, 1)
                                u += 1

                with nc.Block() as b:
                    m0, v0 = mmS.v, vcS.v

                    @b.tensor
                    def _(t, Anxt=Anxt, Ncur=Ncur):
                        for ch in range(NCHS):
                            if ch >= 2:
                                t.wait_ge(vcS.h, v0 + ch - 1)
                            ins = t.matmul(psP[ch % 2][:, 0:128],
                                           Anxt[:, ch * 128:ch * 128 + 128],
                                           Ncur[:, ch * 128:ch * 128 + 128],
                                           start=True, stop=True, skip_group_check=True)
                            mmS.inc(ins, 1)

                    @b.vector
                    def _(v, Ncur=Ncur, Nnxt=Nnxt):
                        for ch in range(NCHS):
                            v.wait_ge(mmS.h, m0 + ch + 1)
                            ins = v.tensor_add(Nnxt[:, ch * 128:ch * 128 + 128],
                                               psP[ch % 2][:, 0:128],
                                               Ncur[:, ch * 128:ch * 128 + 128])
                            vcS.inc(ins, 1)
                Acur, Anxt = Anxt, Acur
                Bcur, Bnxt = Bnxt, Bcur
                Ncur, Nnxt = Nnxt, Ncur

            Nfin = Ncur

            with nc.Block() as b:
                m0, v0 = mmS.v, vcS.v

                @b.tensor
                def _(t, Nfin=Nfin):
                    u = 0
                    for which in range(2):
                        for ch in range(NCHS):
                            if u >= 2:
                                t.wait_ge(vcS.h, v0 + u - 1)
                            rhs = (vb_tm if which == 0 else kb_tm)[:, ch * 256:ch * 256 + 256]
                            ins = t.matmul(psP[u % 2][:, 0:256],
                                           Nfin[:, ch * 128:ch * 128 + 128], rhs,
                                           start=True, stop=True, skip_group_check=True)
                            mmS.inc(ins, 1)
                            u += 1

                @b.vector
                def _(v):
                    u = 0
                    for which in range(2):
                        for ch in range(NCHS):
                            v.wait_ge(mmS.h, m0 + u + 1)
                            dst = (u0s if which == 0 else wws)
                            ins = v.tensor_copy(dst[:, ch * 256:ch * 256 + 256],
                                                psP[u % 2][:, 0:256])
                            vcS.inc(ins, 1)
                            u += 1

            with nc.Block() as b:
                m0, v0 = mmS.v, vcS.v

                @b.tensor
                def _(t):
                    u = 0
                    for ch in range(NCHS):
                        for c2 in range(2):
                            if u >= 2:
                                t.wait_ge(vcS.h, v0 + u - 1)
                            ins = t.transpose(psO[u % 2].bitcast(BF16)[:, 0:128],
                                              wws[:, ch * 256 + c2 * 128:ch * 256 + c2 * 128 + 128],
                                              identb[:, :])
                            mmS.inc(ins, 1)
                            u += 1

                @b.vector
                def _(v):
                    u = 0
                    for ch in range(NCHS):
                        for c2 in range(2):
                            v.wait_ge(mmS.h, m0 + u + 1)
                            ins = v.tensor_copy(
                                wTs[:, ch * 256 + c2 * 128:ch * 256 + c2 * 128 + 128],
                                psO[u % 2].bitcast(BF16)[:, 0:128])
                            vcS.inc(ins, 1)
                            u += 1

            with nc.Block() as b:
                m0, v0, e0 = mmS.v, vcS.v, evS.v
                first = (s == 0)

                def M(ch):
                    if first:
                        return 0 if ch == 0 else 2 + 3 * (ch - 1)
                    return 3 * ch

                def V(ch):
                    if first:
                        return 0 if ch == 0 else 2 + 3 * (ch - 1)
                    return 3 * ch

                def E(ch):
                    if first:
                        return max(0, ch - 1)
                    return ch

                @b.tensor
                def _(t, s=s, first=first):
                    for ch in range(NCHS):
                        g = s * NCHS + ch
                        if g > 0:
                            t.wait_ge(vcS.h, v0 + V(ch) + 1)
                            t.wait_ge(evS.h, e0 + E(ch) + 1)
                            ins = None
                            for kd in range(2):
                                ins = t.matmul(psU[ch % 2][:, 0:256],
                                               wTs[:, ch * 256 + kd * 128:ch * 256 + kd * 128 + 128],
                                               S_sb[:, kd * 256:kd * 256 + 256],
                                               start=(kd == 0), stop=(kd == 1),
                                               skip_group_check=True)
                            mmS.inc(ins, 1)
                            t.wait_ge(vcS.h, v0 + V(ch) + 2)
                            ins = None
                            for kd in range(2):
                                ins = t.matmul(S_ps[:, kd * 256:kd * 256 + 256],
                                               kn_tm[:, ch * 256 + kd * 128:ch * 256 + kd * 128 + 128],
                                               u_bf[:, :], start=False, stop=(g == 63),
                                               skip_group_check=True)
                            mmS.inc(ins, 1)
                            ins = None
                            for c2 in range(2):
                                ins = t.matmul(psO[ch % 2][:, 0:256],
                                               qn[:, c2 * ST + ch * 128:c2 * ST + ch * 128 + 128],
                                               S_sb[:, c2 * 256:c2 * 256 + 256],
                                               start=(c2 == 0), stop=False,
                                               skip_group_check=True)
                            ins = t.matmul(psO[ch % 2][:, 0:256],
                                           alTs[:, ch * 128:ch * 128 + 128],
                                           u_bf[:, :], start=False, stop=True,
                                           skip_group_check=True)
                            mmS.inc(ins, 1)
                        else:
                            t.wait_ge(vcS.h, v0 + 1)
                            ins = None
                            for kd in range(2):
                                ins = t.matmul(S_ps[:, kd * 256:kd * 256 + 256],
                                               kn_tm[:, kd * 128:kd * 128 + 128],
                                               u_bf[:, :], start=True, stop=False,
                                               skip_group_check=True)
                            mmS.inc(ins, 1)
                            ins = t.matmul(psO[0][:, 0:256], alTs[:, 0:128],
                                           u_bf[:, :], start=True, stop=True,
                                           skip_group_check=True)
                            mmS.inc(ins, 1)

                @b.vector
                def _(v, s=s, first=first):
                    for ch in range(NCHS):
                        g = s * NCHS + ch
                        if g > 0:
                            v.wait_ge(mmS.h, m0 + M(ch))
                            ins = v.tensor_copy(S_sb[:, 0:256], S_ps[:, 0:256])
                            vcS.inc(ins, 1)
                            v.wait_ge(mmS.h, m0 + M(ch) + 1)
                            ins = v.tensor_sub(u_bf[:, :], u0s[:, ch * 256:ch * 256 + 256],
                                               psU[ch % 2][:, 0:256])
                            vcS.inc(ins, 1)
                            v.wait_ge(mmS.h, m0 + M(ch) + 3)
                            ins = v.tensor_copy(dl_tm[:, ch * 256:ch * 256 + 256],
                                                psO[ch % 2][:, 0:256])
                            vcS.inc(ins, 1)
                        else:
                            ins = v.tensor_copy(u_bf[:, :], u0s[:, 0:256])
                            vcS.inc(ins, 1)
                            v.wait_ge(mmS.h, m0 + 2)
                            ins = v.tensor_copy(dl_tm[:, 0:256], psO[0][:, 0:256])
                            vcS.inc(ins, 1)

                @b.scalar
                def _(s_, s=s, first=first):
                    for ch in range(NCHS):
                        g = s * NCHS + ch
                        if g > 0:
                            s_.wait_ge(mmS.h, m0 + M(ch))
                            evS.inc(s_.activation(S_sb[:, 256:512], S_ps[:, 256:512],
                                                  AF.Copy), 1)

            with nc.Block() as b:
                m0, v0, e0 = mmS.v, vcS.v, evS.v

                @b.vector
                def _(v):
                    for j in range(64):
                        ins = v.tensor_scalar(dfl[:, j * 128:(j + 1) * 128], identf[:, :],
                                              flgs[:, j:j + 1], None, ALU.mult)
                    vcS.inc(ins, 1)
                    for nt in range(2):
                        v.wait_ge(mmS.h, m0 + nt + 1)
                        ins = v.tensor_copy(ll_fm[:, nt * 512:nt * 512 + 512],
                                            psP[nt % 2][:, 0:512])
                    vcS.inc(ins, 1)
                    for j in range(64):
                        ins = v.tensor_scalar(dfl[:, j * 128:(j + 1) * 128], identf[:, :],
                                              flgs[:, 64 + j:64 + j + 1], None, ALU.mult)
                    vcS.inc(ins, 1)
                    for nt in range(2):
                        v.wait_ge(mmS.h, m0 + 2 + nt + 1)
                        ins = v.tensor_copy(ll_fm[:, ST + nt * 512:ST + nt * 512 + 512],
                                            psP[nt % 2][:, 0:512])
                    vcS.inc(ins, 1)
                    for u in range(4):
                        c2, nt = u // 2, u % 2
                        v.wait_ge(mmS.h, m0 + 4 + u + 1)
                        ins = v.tensor_copy(ls_fm[:, c2 * ST + nt * 512:c2 * ST + nt * 512 + 512],
                                            psP[u % 2][:, 0:512])
                        vcS.inc(ins, 1)
                    v.wait_ge(mmS.h, m0 + 8)
                    for c2 in range(2):
                        ins = v.tensor_copy(vext[:, c2 * (HB + ST):c2 * (HB + ST) + HB],
                                            vext[:, c2 * (HB + ST) + ST:c2 * (HB + ST) + ST + HB])
                    vcS.inc(ins, 1)

                @b.tensor
                def _(t):
                    for c2 in range(2):
                        t.wait_ge(vcS.h, v0 + (1 if c2 == 0 else 3))
                        for nt in range(2):
                            u = c2 * 2 + nt
                            if u >= 2:
                                t.wait_ge(vcS.h, v0 + 2)
                            pp = psP[nt % 2]
                            ins = None
                            for tap in range(64):
                                ins = t.matmul(pp[:, 0:512],
                                               dfl[:, tap * 128:tap * 128 + 128],
                                               vext[:, c2 * (HB + ST) + tap + nt * 512:
                                                    c2 * (HB + ST) + tap + nt * 512 + 512],
                                               start=(tap == 0), stop=(tap == 63),
                                               skip_group_check=True)
                            mmS.inc(ins, 1)
                    for u in range(4):
                        c2, nt = u // 2, u % 2
                        if u < 2:
                            t.wait_ge(vcS.h, v0 + 4)
                        else:
                            t.wait_ge(vcS.h, v0 + 4 + (u - 2) + 1)
                        pp = psP[u % 2]
                        ins = None
                        for tap in range(5):
                            ins = t.matmul(pp[:, 0:512],
                                           dfs[:, (c2 * 5 + tap) * 128:(c2 * 5 + tap) * 128 + 128],
                                           vext[:, c2 * (HB + ST) + 59 + tap + nt * 512:
                                                c2 * (HB + ST) + 59 + tap + nt * 512 + 512],
                                           start=(tap == 0), stop=(tap == 4),
                                           skip_group_check=True)
                        mmS.inc(ins, 1)

            with nc.Block() as b:
                m0, v0 = mmS.v, vcS.v

                @b.tensor
                def _(t):
                    u = 0
                    for (srcp, dstp) in [(ls_fm, ls_tm), (ll_fm, ll_tm)]:
                        for c2 in range(2):
                            for tt in range(8):
                                if u >= 2:
                                    t.wait_ge(vcS.h, v0 + u - 1)
                                ins = t.transpose(psO[u % 2].bitcast(BF16)[:, 0:128],
                                                  srcp[:, c2 * ST + tt * 128:c2 * ST + tt * 128 + 128],
                                                  identb[:, :])
                                mmS.inc(ins, 1)
                                u += 1

                @b.vector
                def _(v):
                    u = 0
                    for (srcp, dstp) in [(ls_fm, ls_tm), (ll_fm, ll_tm)]:
                        for c2 in range(2):
                            for tt in range(8):
                                v.wait_ge(mmS.h, m0 + u + 1)
                                ins = v.tensor_copy(
                                    dstp[:, tt * 256 + c2 * 128:tt * 256 + c2 * 128 + 128],
                                    psO[u % 2].bitcast(BF16)[:, 0:128])
                                vcS.inc(ins, 1)
                                u += 1

            with nc.Block() as b:
                v0 = vcS.v

                @b.vector
                def _(v):
                    comps = [ls_tm, ll_tm, dl_tm, v_tm]
                    for c, comp in enumerate(comps):
                        ap3 = comp[:, :].rearrange("p (tt c_) -> p tt c_", c_=256)
                        v.tensor_reduce(sums[:, (c * 3 + 0) * 8:(c * 3 + 0) * 8 + 8],
                                        ap3, AX.X, ALU.add)
                        v.tensor_reduce(sums[:, (c * 3 + 1) * 8:(c * 3 + 1) * 8 + 8],
                                        ap3, AX.X, ALU.add, apply_absolute_value=True)
                        for tt in range(8):
                            sl = comp[:, tt * 256:tt * 256 + 256]
                            v.scalar_tensor_tensor(
                                scr[:, 0:256], sl, 1.0, sl, ALU.mult, ALU.mult,
                                accum_out=sums[:, (c * 3 + 2) * 8 + tt:(c * 3 + 2) * 8 + tt + 1])
                    stv = st_all[:, :].rearrange("p (tt s_) -> p tt s_", s_=16)
                    for c in range(4):
                        sx = sums[:, (c * 3 + 0) * 8:(c * 3 + 0) * 8 + 8]
                        sa = sums[:, (c * 3 + 1) * 8:(c * 3 + 1) * 8 + 8]
                        s2 = sums[:, (c * 3 + 2) * 8:(c * 3 + 2) * 8 + 8]
                        v.tensor_scalar(stv[:, :, c * 4 + 0], sx, 1.0 / 256, None, ALU.mult)
                        v.tensor_scalar(stv[:, :, c * 4 + 2], sa, 1.0 / 256, None, ALU.mult)
                        v.scalar_tensor_tensor(scr[:, 0:8], stv[:, :, c * 4 + 0], 1.0,
                                               stv[:, :, c * 4 + 0], ALU.mult, ALU.mult)
                        v.scalar_tensor_tensor(stv[:, :, c * 4 + 1], s2, 1.0 / 256,
                                               scr[:, 0:8], ALU.mult, ALU.subtract)
                    ins = v.engine_nop()
                    vcS.inc(ins, 1)

            with nc.Block() as b:
                e0, v0, m0 = evS.v, vcS.v, mmS.v

                @b.scalar
                def _(s_):
                    stv = st_all[:, :].rearrange("p (tt s_) -> p tt s_", s_=16)
                    ins = None
                    for c in range(4):
                        s2 = sums[:, (c * 3 + 2) * 8:(c * 3 + 2) * 8 + 8]
                        s_.activation(scr[:, 0:8], s2, AF.Ln)
                        ins = s_.activation(stv[:, :, c * 4 + 3], scr[:, 0:8], AF.Exp, scale=0.5)
                    evS.inc(ins, 1)

                @b.vector
                def _(v):
                    v.wait_ge(evS.h, e0 + 1)
                    for tt in range(8):
                        ins = v.tensor_copy(st_bf[:, :], st_all[:, tt * 16:tt * 16 + 16])
                        vcS.inc(ins, 1)
                        v.wait_ge(mmS.h, m0 + tt + 1)
                        ins = v.tensor_copy(stf[0:16, tt * 128:tt * 128 + 128],
                                            psP[tt % 2].bitcast(BF16)[0:16, 0:128])
                        vcS.inc(ins, 1)

                @b.tensor
                def _(t):
                    for tt in range(8):
                        t.wait_ge(vcS.h, v0 + 2 * tt + 1)
                        ins = t.transpose(psP[tt % 2].bitcast(BF16)[0:16, 0:128],
                                          st_bf[:, :], identb[:, :])
                        mmS.inc(ins, 1)

            with nc.Block() as b:
                m0, v0, e0 = mmS.v, vcS.v, evS.v

                @b.tensor
                def _(t):
                    u = 0
                    for Mt in range(8):
                        for nt in range(2):
                            if u >= 2:
                                t.wait_ge(vcS.h, v0 + u - 1)
                            ins = t.matmul(psP[u % 2][:, 0:512],
                                           w1ss[0:16, Mt * 128:Mt * 128 + 128],
                                           stf[0:16, nt * 512:nt * 512 + 512],
                                           start=True, stop=True, skip_group_check=True)
                            mmS.inc(ins, 1)
                            u += 1

                @b.vector
                def _(v):
                    u = 0
                    for Mt in range(8):
                        for nt in range(2):
                            v.wait_ge(mmS.h, m0 + u + 1)
                            if u >= 1:
                                v.wait_ge(evS.h, e0 + u)
                            ins = v.tensor_add(scr[:, 0:512], psP[u % 2][:, 0:512],
                                               bsl8[:, Mt * ST + nt * 512:Mt * ST + nt * 512 + 512])
                            vcS.inc(ins, 1)
                            u += 1

                @b.scalar
                def _(s_):
                    u = 0
                    for Mt in range(8):
                        for nt in range(2):
                            s_.wait_ge(vcS.h, v0 + u + 1)
                            ins = s_.activation(h1f[:, Mt * ST + nt * 512:Mt * ST + nt * 512 + 512],
                                                scr[:, 0:512], AF.Gelu)
                            evS.inc(ins, 1)
                            u += 1

            with nc.Block() as b:
                m0, v0 = mmS.v, vcS.v

                @b.tensor
                def _(t):
                    for tt in range(8):
                        if tt >= 2:
                            t.wait_ge(vcS.h, v0 + tt - 1)
                        ins = None
                        for k in range(8):
                            ins = t.matmul(psP[tt % 2][:, 0:4],
                                           h1f[:, k * ST + tt * 128:k * ST + tt * 128 + 128],
                                           w2s[:, k * 4:k * 4 + 4],
                                           start=(k == 0), stop=False, skip_group_check=True)
                        ins = t.matmul(psP[tt % 2][:, 0:4], onesb[0:1, :], b2s[0:1, :],
                                       start=False, stop=True, skip_group_check=True)
                        mmS.inc(ins, 1)

                @b.vector
                def _(v):
                    for tt in range(8):
                        v.wait_ge(mmS.h, m0 + tt + 1)
                        ins = v.tensor_copy(logit[:, tt * 4:tt * 4 + 4], psP[tt % 2][:, 0:4])
                        vcS.inc(ins, 1)

            with nc.Block() as b:
                @b.vector
                def _(v):
                    v.tensor_scalar(zb32[:, :], logit[:, :], invT[:, 0:1], None, ALU.mult)
            with nc.Block() as b:
                @b.vector
                def _(v):
                    v.tensor_reduce(mx8[:, :], zb32[:, :].rearrange("p (tt c) -> p tt c", c=4),
                                    AX.X, ALU.max)
            with nc.Block() as b:
                @b.vector
                def _(v):
                    v.tensor_scalar(nmx8[:, :], mx8[:, :], -1.0, None, ALU.mult)
            with nc.Block() as b:
                @b.scalar
                def _(s_):
                    for tt in range(8):
                        s_.activation(eb32[:, tt * 4:tt * 4 + 4], zb32[:, tt * 4:tt * 4 + 4],
                                      AF.Exp, bias=nmx8[:, tt:tt + 1])
            with nc.Block() as b:
                @b.vector
                def _(v):
                    v.tensor_reduce(mx8[:, :], eb32[:, :].rearrange("p (tt c) -> p tt c", c=4),
                                    AX.X, ALU.add)
            with nc.Block() as b:
                @b.vector
                def _(v):
                    v.reciprocal(nmx8[:, :], mx8[:, :])
            with nc.Block() as b:
                @b.vector
                def _(v):
                    for tt in range(8):
                        v.scalar_tensor_tensor(zb32[:, tt * 4:tt * 4 + 4],
                                               eb32[:, tt * 4:tt * 4 + 4],
                                               nmx8[:, tt:tt + 1],
                                               eb32[:, tt * 4:tt * 4 + 4],
                                               ALU.mult, ALU.bypass)
            with nc.Block() as b:
                v0 = vcS.v

                @b.vector
                def _(v):
                    for tt in range(8):
                        ls_t = ls_tm[:, tt * 256:tt * 256 + 256]
                        ll_t = ll_tm[:, tt * 256:tt * 256 + 256]
                        dl_t = dl_tm[:, tt * 256:tt * 256 + 256]
                        v_t = v_tm[:, tt * 256:tt * 256 + 256]
                        v.tensor_add(oacc[:, :], ls_t, ll_t)
                        v.tensor_add(oacc[:, :], oacc[:, :], dl_t)
                        v.tensor_add(oacc[:, :], oacc[:, :], v_t)
                        v.tensor_scalar(oacc[:, :], oacc[:, :], 0.05, None, ALU.mult)
                        v.tensor_scalar(oacc2[:, :], ls_t, zb32[:, tt * 4:tt * 4 + 1],
                                        None, ALU.mult)
                        v.scalar_tensor_tensor(oacc2[:, :], ll_t, zb32[:, tt * 4 + 1:tt * 4 + 2],
                                               oacc2[:, :], ALU.mult, ALU.add)
                        v.scalar_tensor_tensor(oacc2[:, :], dl_t, zb32[:, tt * 4 + 2:tt * 4 + 3],
                                               oacc2[:, :], ALU.mult, ALU.add)
                        v.scalar_tensor_tensor(oacc2[:, :], v_t, zb32[:, tt * 4 + 3:tt * 4 + 4],
                                               oacc2[:, :], ALU.mult, ALU.add)
                        v.scalar_tensor_tensor(omx[:, tt * 256:tt * 256 + 256], oacc2[:, :],
                                               0.8, oacc[:, :], ALU.mult, ALU.add)
            with nc.Block() as b:
                @b.vector
                def _(v):
                    for tt in range(8):
                        v.scalar_tensor_tensor(scr[:, 0:256], omx[:, tt * 256:tt * 256 + 256],
                                               1.0, omx[:, tt * 256:tt * 256 + 256],
                                               ALU.mult, ALU.mult,
                                               accum_out=mx8[:, tt:tt + 1])
            with nc.Block() as b:
                @b.vector
                def _(v):
                    v.tensor_scalar(nmx8[:, :], mx8[:, :], 1.0 / 256, 1e-5, ALU.mult, ALU.add)
            with nc.Block() as b:
                @b.scalar
                def _(s_):
                    s_.activation(mx8[:, :], nmx8[:, :], AF.Ln)
            with nc.Block() as b:
                @b.scalar
                def _(s_):
                    s_.activation(nmx8[:, :], mx8[:, :], AF.Exp, scale=-0.5)
            with nc.Block() as b:
                d0 = dl.v

                @b.vector
                def _(v):
                    for tt in range(8):
                        ins = v.tensor_scalar(ot_b8[:, tt * 256:tt * 256 + 256],
                                              omx[:, tt * 256:tt * 256 + 256],
                                              nmx8[:, tt:tt + 1], None, ALU.mult)
                        vcS.inc(ins, 1)

                @b.gpsimd
                def _(g, S0=S0):
                    g.wait_ge(vcS.h, vcS.v)
                    dl.inc(g.dma_start(
                        out=onrm.ap()[S0:S0 + ST, :].rearrange("(tt p) c -> p tt c", p=128),
                        in_=ot_b8[:, :].rearrange("p (tt c) -> p tt c", c=256)), 16)
                    g.wait_ge(dl.h, dl.v)

        if debug_outs:
            dbg = {}
            for nm, t_, sh, dt_ in [("g_qn", qn, [128, 2 * ST], BF16), ("g_kn", kn, [128, 2 * ST], BF16),
                                    ("g_vb", vb, [128, 2 * ST], BF16), ("g_dl", dl_tm, [128, 8 * 256], BF16),
                                    ("g_ls", ls_tm, [128, 8 * 256], BF16), ("g_ll", ll_tm, [128, 8 * 256], BF16),
                                    ("g_vt", v_tm, [128, 8 * 256], BF16), ("g_st", st_all, [128, 8 * 16], F32),
                                    ("g_h1", h1f, [128, 8 * ST], BF16), ("g_lg", logit, [128, 8 * 4], F32),
                                    ("g_u0", u0s, [128, 8 * 256], BF16), ("g_ww", wws, [128, 8 * 256], BF16),
                                    ("g_al", alTs, [128, 8 * 128], BF16), ("g_ssb", S_sb, [128, 512], BF16),
                                    ("g_brow", brow, [1, ST], F32), ("g_stf", stf, [16, ST], BF16),
                                    ("g_qs", qs_, [128, 2 * ST], BF16), ("g_rq", rq, [1, ST], F32),
                                    ("g_rk", rk, [1, ST], F32),
                                    ("g_oa2", oacc2, [128, 256], F32),
                                    ("g_it", invT, [128, 1], F32), ("g_qraw", qraw, [128, 2 * (ST + 3)], BF16),
                                    ("g_kraw", kraw, [128, 2 * (ST + 3)], BF16)]:
                o = nc.dram_tensor(nm, sh, dt_, kind="ExternalOutput")
                dbg[nm] = (o, t_)
            with nc.Block() as b:
                @b.gpsimd
                def _(g):
                    for nm, (o, t_) in dbg.items():
                        dl.inc(g.dma_start(out=o.ap(), in_=t_[:, :] if not isinstance(t_, bass.AP) else t_), 16)
                    g.wait_ge(dl.h, dl.v)
        es.close()
        return nc

    return build_B

build_B = _mk_B()

def _mk_C():

    F32 = mybir.dt.float32
    BF16 = mybir.dt.bfloat16
    AF = mybir.ActivationFunctionType
    ALU = mybir.AluOpType
    AX = mybir.AxisListType

    T = 2048
    KC = 8
    NN = 2048


    NBLK = 99
    SUB = 99
    def build_C():
        nc = bass.Bass("TRN2", target_bir_lowering=False)

        def din(name, shape, dt=F32):
            return nc.dram_tensor(name, shape, dt, kind="ExternalInput")

        x1f = din("x1f", [128, KC * T])
        xn1f = din("xn1f", [128, KC * T], BF16)
        o4 = din("o4", [8, T, 128], BF16)
        Wo = din("Wo", [1024, 1024])
        onwc = din("onwc", [128, 8])
        WinP = din("WinP", [1024, NN])
        binP = din("binP", [1, NN])
        WoutP = din("WoutP", [NN, 1024])
        yf = nc.dram_tensor("yf", [128, KC * T], F32, kind="ExternalOutput")

        wo_s = nc.dram_tensor("wo_s", [128, 8 * 8 * 128], BF16)
        wout_s = nc.dram_tensor("wout_s", [128, 8 * 16 * 128], BF16)
        s_dr = nc.dram_tensor("s_dr", [128, KC * T], F32)

        es = ExitStack()

        def sb(name, shape, dt):
            return es.enter_context(nc.sbuf_tensor(name, shape, dt))

        winT = sb("winT", [128, KC * NN], BF16)
        xn1p = sb("xn1p", [128, KC * T], BF16)
        o_fm = sb("o_fm", [128, KC * T], BF16)
        binr = sb("binr", [1, NN], F32)
        onws = sb("onws", [128, 8], F32)
        identf = sb("identf", [128, 128], F32)
        identb = sb("identb", [128, 128], BF16)
        iotp = sb("iotp", [128, 1], F32)
        onesb = sb("onesb", [1, 128], F32)
        wsl0 = sb("wsl0", [128, 8 * 128], BF16)
        wsl1 = sb("wsl1", [128, 8 * 128], BF16)
        wslG = sb("wslG", [128, 16 * 128], BF16)
        wslH = sb("wslH", [128, 16 * 128], BF16)
        acts = [sb(f"acts{i}", [128, NN], BF16) for i in range(4)]
        decs = [sb(f"decs{i}", [128, NN], BF16) for i in range(4)]
        dccs = [sb(f"dccs{i}", [128, NN], BF16) for i in range(4)]
        dmap = [sb(f"dmap{i}", [128, NN], BF16) for i in range(4)]
        mskf = sb("mskf", [128, 16 * 512], BF16)
        mskf_f = mskf.bitcast(F32)
        lgb0 = mskf_f[:, 0:512]
        lgb1 = mskf_f[:, 512:1024]
        ev0 = sb("ev0", [128, T], F32)
        biasb = ev0
        mskf_f = None
        o_fm_f = o_fm.bitcast(F32)
        gb = o_fm_f[:, 0:512]
        xp0 = o_fm_f[:, 512:1024]; xp1 = o_fm_f[:, 1024:1536]
        sp0 = o_fm_f[:, 1536:2048]; sp1 = o_fm_f[:, 2048:2560]
        yp0 = o_fm_f[:, 2560:3072]; yp1 = o_fm_f[:, 3072:3584]

        psA = es.enter_context(nc.psum_tensor("psA", [128, 512], F32))
        psB = es.enter_context(nc.psum_tensor("psB", [128, 512], F32))
        psC = es.enter_context(nc.psum_tensor("psC", [128, 512], F32))
        psD = es.enter_context(nc.psum_tensor("psD", [128, 512], F32))
        PS = [psA, psB, psC, psD]

        class Sm:
            def __init__(self, name):
                self.h = es.enter_context(nc.semaphore(name))
                self.v = 0

            def inc(self, instr, n=1):
                instr.then_inc(self.h, n)
                self.v += n

        dl = Sm("dl"); dls = Sm("dls"); mmS = Sm("mmS"); evS = Sm("evS"); vcS = Sm("vcS")

        def stash_slab_ap(stash, mt, kt, m):
            return stash.ap().rearrange("p (m k c) -> p m k c", m=mt, k=kt)[:, m, :, :]

        def dr_slab(drt, width, c, w0=None, w1=None):
            ap = drt.ap().rearrange("p (c w) -> p c w", w=width)[:, c, :]
            if w0 is not None:
                ap = ap[:, w0:w1]
            return ap

        with nc.Block() as b:
            @b.gpsimd
            def _(g):
                dl.inc(g.dma_start(out=onws[:, :], in_=onwc.ap()), 16)
                dl.inc(g.dma_start(out=binr[:, :], in_=binP.ap()), 16)
                for k in range(KC):
                    dl.inc(g.dma_start(out=winT[:, k * NN:(k + 1) * NN],
                                       in_=WinP.ap()[k * 128:(k + 1) * 128, :]), 16)
                dl.inc(g.dma_start(out=xn1p[:, :], in_=xn1f.ap()), 16)
                g.iota(identf[:, :], [[1, 128]], channel_multiplier=0,
                       allow_small_or_imprecise_dtypes=True)
                g.iota(iotp[:, :1], [[1, 1]], channel_multiplier=1,
                       allow_small_or_imprecise_dtypes=True)
                g.memset(onesb[:, :], 1.0)
                for i in range(4):
                    g.memset(dmap[i][:, :], 0.0)
                do = wo_s.ap().rearrange("p (m k c) -> p m k c", m=8, k=8)
                si = Wo.ap().rearrange("(k p) (m c) -> p m k c", p=128, c=128)
                dl.inc(g.dma_start(out=do, in_=si), 16)
                do = wout_s.ap().rearrange("p (m k c) -> p m k c", m=8, k=16)
                si = WoutP.ap().rearrange("(k p) (m c) -> p m k c", p=128, c=128)
                for m in range(8):
                    dl.inc(g.dma_start(out=do[:, m], in_=si[:, m]), 16)
                g.wait_ge(dl.h, dl.v)

            @b.sync
            def _(sy):
                for hc in range(8):
                        dls.inc(sy.dma_start(
                            out=o_fm[:, hc * T:(hc + 1) * T],
                            in_=o4.ap()[hc, :, :], transpose=True), 16)
                sy.wait_ge(dls.h, dls.v)

        with nc.Block() as b:
            @b.vector
            def _(v):
                v.tensor_scalar(identf[:, :], identf[:, :], iotp[:, :1], None, ALU.is_equal)
                v.tensor_copy(identb[:, :], identf[:, :])
                for c in range(KC):
                    v.tensor_scalar(o_fm[:, c * T:(c + 1) * T], o_fm[:, c * T:(c + 1) * T],
                                    onws[:, c:c + 1], None, ALU.mult)

        if NBLK <= 1:
            es.close()
            return nc
        with nc.Block() as b:
            g0, m0, v0 = dl.v, mmS.v, vcS.v
            wl_done, st_done = {}, {}
            dv = g0
            for m in range(8):
                dv += 16
                wl_done[m] = dv
                if m >= 1:
                    dv += 16
                    st_done[m - 1] = dv
            dv += 16
            st_done[7] = dv

            @b.gpsimd
            def _(g):
                for m in range(8):
                    if m >= 2:
                        g.wait_ge(mmS.h, m0 + 4 * (m - 1))
                    dl.inc(g.dma_start(out=(wsl0 if m % 2 == 0 else wsl1)[:, :],
                                       in_=stash_slab_ap(wo_s, 8, 8, m)), 16)
                    if m >= 1:
                        g.wait_ge(vcS.h, v0 + m)
                        dl.inc(g.dma_start(out=dr_slab(s_dr, T, m - 1), in_=ev0[:, :]), 16)
                g.wait_ge(vcS.h, v0 + 8)
                dl.inc(g.dma_start(out=dr_slab(s_dr, T, 7), in_=ev0[:, :]), 16)
                g.wait_ge(dl.h, dl.v)

            @b.tensor
            def _(t):
                for m in range(8):
                    t.wait_ge(dl.h, wl_done[m])
                    if m >= 1:
                        t.wait_ge(vcS.h, v0 + m)
                    wsl_ = wsl0 if m % 2 == 0 else wsl1
                    for nt in range(4):
                        pp = PS[nt]
                        ins = None
                        for k in range(KC):
                            ins = t.matmul(pp[:, :], wsl_[:, k * 128:(k + 1) * 128],
                                           o_fm[:, k * T + nt * 512:k * T + nt * 512 + 512],
                                           start=(k == 0), stop=(k == KC - 1),
                                           skip_group_check=True)
                        mmS.inc(ins, 1)

            @b.vector
            def _(v):
                for m in range(8):
                    v.wait_ge(mmS.h, m0 + 4 * (m + 1))
                    if m >= 1:
                        v.wait_ge(dl.h, st_done[m - 1])
                    ins = None
                    for nt in range(4):
                        ins = v.tensor_copy(ev0[:, nt * 512:(nt + 1) * 512], PS[nt][:, :])
                    vcS.inc(ins, 1)

        if NBLK <= 2:
            es.close()
            return nc
        with nc.Block() as b:
            m0 = mmS.v

            @b.tensor
            def _(t):
                for nt in range(4):
                    mmS.inc(t.matmul(PS[nt][:, :], onesb[0:1, :],
                                     binr[0:1, nt * 512:nt * 512 + 512],
                                     start=True, stop=True), 1)

            @b.vector
            def _(v):
                for nt in range(4):
                    v.wait_ge(mmS.h, m0 + nt + 1)
                    v.tensor_copy(biasb[:, nt * 512:nt * 512 + 512], PS[nt][:, :])
        for grp in range(min(4, NBLK - 2)):
            tts = [grp * 4 + i for i in range(4)]
            with nc.Block() as b:
                m0, e0, v0 = mmS.v, evS.v, vcS.v

                @b.tensor
                def _(t, tts=tts):
                    u = 0
                    for i, tt in enumerate(tts):
                        for nt in range(4):
                            pp = PS[u % 4]
                            if u >= 4:
                                t.wait_ge(vcS.h, v0 + u - 3)
                            ins = None
                            for k in range(KC):
                                ins = t.matmul(pp[:, :],
                                               xn1p[:, k * T + tt * 128:k * T + tt * 128 + 128],
                                               winT[:, k * NN + nt * 512:k * NN + nt * 512 + 512],
                                               start=(k == 0), stop=(k == KC - 1),
                                               skip_group_check=True)
                            mmS.inc(ins, 1)
                            u += 1

                @b.vector
                def _(v, tts=tts):
                    u = 0
                    for i, tt in enumerate(tts):
                        for nt in range(4):
                            v.wait_ge(mmS.h, m0 + u + 1)
                            if u >= 2:
                                v.wait_ge(evS.h, e0 + u - 1)
                            lg = lgb0 if u % 2 == 0 else lgb1
                            ins = v.tensor_tensor(lg[:, :], PS[u % 4][:, :],
                                                  biasb[:, nt * 512:nt * 512 + 512], ALU.add)
                            vcS.inc(ins, 1)
                            v.tensor_scalar(decs[i][:, nt * 512:nt * 512 + 512],
                                            lg[:, :], 0.0, None, ALU.is_gt)
                            v.tensor_scalar(dccs[i][:, nt * 512:nt * 512 + 512],
                                            lg[:, :], 0.0, None, ALU.is_le)
                            u += 1

                @b.scalar
                def _(s_, tts=tts):
                    u = 0
                    for i, tt in enumerate(tts):
                        for nt in range(4):
                            s_.wait_ge(vcS.h, v0 + u + 1)
                            lg = lgb0 if u % 2 == 0 else lgb1
                            evS.inc(s_.activation(acts[i][:, nt * 512:nt * 512 + 512],
                                                  lg[:, :], AF.Silu), 1)
                            u += 1

            if SUB <= 1:
                break
            with nc.Block() as b:
                @b.gpsimd
                def _(g):
                    for i in range(4):
                        g.memset(dmap[i][:, 0:8 * 255].rearrange("p (tr n) -> p tr n", n=255)[:, :, 0:1], 1.0)

            if SUB <= 2:
                break
            for d in range(min(7, SUB - 2)):
                lvl = 2 ** d
                with nc.Block() as b:
                    @b.vector
                    def _(v, d=d, lvl=lvl):
                        for i in range(4):
                            dm3 = dmap[i][:, 0:8 * 255].rearrange("p (tr n) -> p tr n", n=255)
                            pd = dm3[:, :, lvl - 1:2 * lvl - 1]
                            dl_ = decs[i][:, 0:8 * 255].rearrange("p (tr n) -> p tr n", n=255)[
                                :, :, lvl - 1:2 * lvl - 1]
                            dc_ = dccs[i][:, 0:8 * 255].rearrange("p (tr n) -> p tr n", n=255)[
                                :, :, lvl - 1:2 * lvl - 1]
                            nxt = dm3[:, :, 2 * lvl - 1:4 * lvl - 1]
                            nxt2 = nxt.rearrange("p tr (n two) -> p tr n two", two=2)
                            v.tensor_tensor(nxt2[:, :, :, 1], pd, dl_, ALU.mult)
                            v.tensor_tensor(nxt2[:, :, :, 0], pd, dc_, ALU.mult)

            if SUB <= 9:
                break
            with nc.Block() as b:
                @b.vector
                def _(v):
                    for i in range(4):
                        v.tensor_mul(acts[i][:, :], acts[i][:, :], dmap[i][:, :])

            if SUB <= 10:
                break
            with nc.Block() as b:
                m0, v0 = mmS.v, vcS.v

                @b.tensor
                def _(t):
                    u = 0
                    for i in range(4):
                        for kcn in range(16):
                            if u >= 2:
                                t.wait_ge(vcS.h, v0 + u - 1)
                            ins = t.transpose(PS[u % 2].bitcast(BF16)[:, 0:128],
                                              acts[i][:, kcn * 128:kcn * 128 + 128],
                                              identb[:, :])
                            mmS.inc(ins, 1)
                            u += 1

                @b.vector
                def _(v):
                    u = 0
                    for i in range(4):
                        for kcn in range(16):
                            v.wait_ge(mmS.h, m0 + u + 1)
                            ins = v.tensor_copy(
                                mskf[:, kcn * 512 + i * 128:kcn * 512 + i * 128 + 128],
                                PS[u % 2].bitcast(BF16)[:, 0:128])
                            vcS.inc(ins, 1)
                            u += 1

            if SUB <= 11:
                break
            with nc.Block() as b:
                g0, s0, m0, e0, v0 = dl.v, dls.v, mmS.v, evS.v, vcS.v
                wl_done, st2 = {}, {}
                dv = g0
                for mg in range(4):
                    dv += 32
                    wl_done[mg] = dv
                    if mg >= 1:
                        dv += 16
                        st2[(mg - 1, 0)] = dv
                        dv += 16
                        st2[(mg - 1, 1)] = dv
                dv += 16
                st2[(3, 0)] = dv
                dv += 16
                st2[(3, 1)] = dv

                @b.gpsimd
                def _(g, grp=grp):
                    wo4 = wout_s.ap().rearrange("p (m k c) -> p m k c", m=8, k=16)
                    for mg in range(4):
                        if mg >= 1:
                            g.wait_ge(mmS.h, m0 + 32 * mg)
                        dl.inc(g.dma_start(out=(wslG if mg % 2 == 0 else wslH)[:, :],
                                           in_=wo4[:, 2 * mg, :, :]), 16)
                        dl.inc(g.dma_start(out=(wslH if mg % 2 == 0 else wslG)[:, :],
                                           in_=wo4[:, 2 * mg + 1, :, :]), 16)
                        if mg >= 1:
                            g.wait_ge(vcS.h, v0 + 2 * (mg - 1) + 1)
                            dl.inc(g.dma_start(
                                out=dr_slab(yf, T, 2 * (mg - 1))[:, grp * 512:grp * 512 + 512],
                                in_=yp0[:, :]), 16)
                            g.wait_ge(vcS.h, v0 + 2 * (mg - 1) + 2)
                            dl.inc(g.dma_start(
                                out=dr_slab(yf, T, 2 * (mg - 1) + 1)[:, grp * 512:grp * 512 + 512],
                                in_=yp1[:, :]), 16)
                    g.wait_ge(vcS.h, v0 + 7)
                    dl.inc(g.dma_start(out=dr_slab(yf, T, 6)[:, grp * 512:grp * 512 + 512],
                                       in_=yp0[:, :]), 16)
                    g.wait_ge(vcS.h, v0 + 8)
                    dl.inc(g.dma_start(out=dr_slab(yf, T, 7)[:, grp * 512:grp * 512 + 512],
                                       in_=yp1[:, :]), 16)
                    g.wait_ge(dl.h, dl.v)

                @b.sync
                def _(sy, grp=grp):
                    for mg in range(4):
                        for i in range(2):
                            mo = 2 * mg + i
                            if mg >= 1:
                                sy.wait_ge(vcS.h, v0 + 2 * (mg - 1) + i + 1)
                            dls.inc(sy.dma_start(
                                out=(xp0 if i == 0 else xp1)[:, :],
                                in_=dr_slab(x1f, T, mo)[:, grp * 512:grp * 512 + 512]), 16)
                            dls.inc(sy.dma_start(
                                out=(sp0 if i == 0 else sp1)[:, :],
                                in_=dr_slab(s_dr, T, mo)[:, grp * 512:grp * 512 + 512]), 16)

                @b.tensor
                def _(t):
                    for mg in range(4):
                        t.wait_ge(dl.h, wl_done[mg])
                        if mg >= 2:
                            t.wait_ge(evS.h, e0 + 2 * (mg - 1))
                        ins = None
                        for i in range(2):
                            ws_ = ((wslG if mg % 2 == 0 else wslH) if i == 0 else
                                   (wslH if mg % 2 == 0 else wslG))
                            pp = PS[2 * (mg % 2) + i]
                            for k in range(16):
                                ins = t.matmul(pp[:, :], ws_[:, k * 128:(k + 1) * 128],
                                               mskf[:, k * 512:k * 512 + 512],
                                               start=(k == 0), stop=(k == 15),
                                               skip_group_check=True)
                            mmS.inc(ins, 16)

                @b.scalar
                def _(s_):
                    for mg in range(4):
                        for i in range(2):
                            uu = 2 * mg + i
                            s_.wait_ge(mmS.h, m0 + 16 * (uu + 1))
                            if uu >= 1:
                                s_.wait_ge(vcS.h, v0 + uu)
                            pp = PS[2 * (mg % 2) + i]
                            evS.inc(s_.activation(gb[:, :], pp[:, :], AF.Silu), 1)

                @b.vector
                def _(v):
                    for mg in range(4):
                        for i in range(2):
                            uu = 2 * mg + i
                            v.wait_ge(evS.h, e0 + uu + 1)
                            v.wait_ge(dls.h, s0 + 16 * (4 * mg + 2 * i + 2))
                            if mg >= 1:
                                v.wait_ge(dl.h, st2[(mg - 1, i)])
                            sp = sp0 if i == 0 else sp1
                            xp = xp0 if i == 0 else xp1
                            yp = yp0 if i == 0 else yp1
                            v.tensor_mul(yp[:, :], gb[:, :], sp[:, :])
                            ins = v.tensor_add(yp[:, :], yp[:, :], xp[:, :])
                            vcS.inc(ins, 1)

        es.close()
        return nc

    return build_C

build_C = _mk_C()



_CACHE = {}


def _progs():
    if "A" not in _CACHE:
        _CACHE["A"] = build_A()
        _CACHE["B"] = build_B()
        _CACHE["C"] = build_C()
    return _CACHE["A"], _CACHE["B"], _CACHE["C"]


def _prep_A(I):
    x = np.asarray(I["L0_attn_x"])
    sh = {}
    sh["w1c"] = np.ascontiguousarray(np.asarray(I["L0_attn_norm1_w"]).reshape(8, 128).T)
    sh["w2c"] = np.ascontiguousarray(np.asarray(I["L0_attn_norm2_w"]).reshape(8, 128).T)
    sh["bdc"] = np.ascontiguousarray(np.asarray(I["L0_attn_bdecay"]).reshape(8, 128).T)
    sh["bgc"] = np.ascontiguousarray(np.asarray(I["L0_attn_bgate"]).reshape(8, 128).T)
    sh["bcc"] = np.ascontiguousarray(np.asarray(I["L0_attn_bc"]).reshape(8, 128).T)
    sh["b1c"] = np.ascontiguousarray(np.asarray(I["L1_attn_mlp_b1"]).reshape(8, 128).T)
    sh["sscv"] = np.broadcast_to(np.asarray(I["L0_attn_state_scale"], np.float32), (128, 1)).copy()
    sh["mscv"] = np.broadcast_to(np.asarray(I["L0_attn_mem_scale"], np.float32), (128, 1)).copy()
    sh["cwc"] = np.ascontiguousarray(
        np.asarray(I["L0_attn_conv_w"]).reshape(8, 128, 7).transpose(1, 0, 2).reshape(128, 56))
    sh["Wd"] = np.ascontiguousarray(np.asarray(I["L0_attn_Wdecay"]))
    sh["Wgt"] = np.ascontiguousarray(np.asarray(I["L0_attn_Wgate"]))
    sh["Wkk"] = np.ascontiguousarray(np.asarray(I["L0_attn_Wk"]))
    sh["MTd"] = np.ascontiguousarray(np.asarray(I["L0_attn_M"]).T)
    sh["Wm"] = np.ascontiguousarray(np.asarray(I["L0_attn_Wmem"]))
    sh["Wcc"] = np.ascontiguousarray(np.asarray(I["L0_attn_Wc"]))
    sh["Wgg"] = np.ascontiguousarray(np.asarray(I["L0_attn_Wg"]))
    sh["Wup"] = np.ascontiguousarray(np.asarray(I["L0_attn_Wup"]))
    sh["Wdn"] = np.ascontiguousarray(np.asarray(I["L0_attn_Wdown"]))
    sh["WfcT"] = np.ascontiguousarray(np.asarray(I["L0_ffn_W_fc"]).T)
    sh["WpjT"] = np.ascontiguousarray(np.asarray(I["L0_ffn_W_proj"]).T)
    sh["W1h"] = np.ascontiguousarray(np.asarray(I["L1_attn_mlp_w1"])[:1024])
    in_maps = []
    for core in range(8):
        bidx, j = core // 4, core % 4
        sl = x[bidx, 2048 * j: 2048 * (j + 1)]
        m = dict(sh)
        m["xT"] = np.ascontiguousarray(sl.T)
        if j == 0:
            m["xhalo"] = np.zeros((6, 1024), np.float32)
        else:
            m["xhalo"] = np.ascontiguousarray(x[bidx, 2048 * j - 6: 2048 * j])
        m["cdc"] = np.full((128, 1), 1.0 if j == 0 else 0.0, np.float32)
        in_maps.append(m)
    return in_maps


def _prep_B(I, xnfm, bsfm):
    in_maps = []
    for core in range(8):
        b, h = core // 4, core % 4
        m = {}
        m["xnf"] = xnfm[b]
        m["bsf"] = bsfm[b]
        m["wq"] = np.ascontiguousarray(np.asarray(I["L1_attn_Wq"])[:, h * 256:(h + 1) * 256])
        m["wk"] = np.ascontiguousarray(np.asarray(I["L1_attn_Wk"])[:, h * 256:(h + 1) * 256])
        m["wv"] = np.ascontiguousarray(np.asarray(I["L1_attn_Wv"])[:, h * 256:(h + 1) * 256])
        m["wbv"] = np.ascontiguousarray(np.asarray(I["L1_attn_Wb"])[:, h].reshape(8, 128).T)
        m["cwq"] = np.ascontiguousarray(
            np.asarray(I["L1_attn_conv_q_w"])[h * 256:(h + 1) * 256].reshape(2, 128, 4)
            .transpose(1, 0, 2).reshape(128, 8))
        m["cwk"] = np.ascontiguousarray(
            np.asarray(I["L1_attn_conv_k_w"])[h * 256:(h + 1) * 256].reshape(2, 128, 4)
            .transpose(1, 0, 2).reshape(128, 8))
        m["cwv"] = np.ascontiguousarray(
            np.asarray(I["L1_attn_conv_v_w"])[h * 256:(h + 1) * 256].reshape(2, 128, 4)
            .transpose(1, 0, 2).reshape(128, 8))
        m["fsh"] = np.ascontiguousarray(
            np.asarray(I["L1_attn_fir_short_w"])[h].reshape(2, 128, 5)
            .transpose(1, 0, 2).reshape(128, 10))
        m["flg"] = np.ascontiguousarray(
            np.asarray(I["L1_attn_fir_long_w"])[h].reshape(2, 128, 64)
            .transpose(1, 0, 2).reshape(128, 128))
        m["w1sd"] = np.ascontiguousarray(np.asarray(I["L1_attn_mlp_w1"])[1024:1040])
        m["w2d"] = np.ascontiguousarray(np.asarray(I["L1_attn_mlp_w2"]))
        m["b2d"] = np.ascontiguousarray(np.asarray(I["L1_attn_mlp_b2"]).reshape(1, 4))
        m["ltd"] = np.full((128, 1), float(np.asarray(I["L1_attn_log_temp"])[h, 0]), np.float32)
        in_maps.append(m)
    return in_maps


def _prep_C(I, resA, onrm_by_core):
    WinP = np.zeros((1024, 2048), np.float32)
    WinP[:, 0:2040] = np.asarray(I["L1_ffn_W_in"]).T
    binP = np.zeros((1, 2048), np.float32)
    binP[0, 0:2040] = np.asarray(I["L1_ffn_b_in"])
    WoutP = np.zeros((2048, 1024), np.float32)
    WoutP[0:2040] = np.asarray(I["L1_ffn_W_out"]).T
    onw = np.asarray(I["L1_attn_o_norm_w"])
    onwc = np.ascontiguousarray(np.tile(onw, 4).reshape(8, 128).T)
    Wo = np.ascontiguousarray(np.asarray(I["L1_attn_Wo"]))
    in_maps = []
    for core in range(8):
        b, j = core // 4, core % 4
        sl = slice(2048 * j, 2048 * (j + 1))
        m = {}
        m["x1f"] = resA[core]["x1f"]
        m["xn1f"] = resA[core]["xn1f"]
        o_stack = np.stack([np.asarray(onrm_by_core[4 * b + h])[sl] for h in range(4)])
        m["o4"] = np.ascontiguousarray(
            o_stack.reshape(4, 2048, 2, 128).transpose(0, 2, 1, 3).reshape(8, 2048, 128))
        m["Wo"] = Wo
        m["onwc"] = onwc
        m["WinP"] = WinP
        m["binP"] = binP
        m["WoutP"] = WoutP
        in_maps.append(m)
    return in_maps


def kernel(**inputs):
    ncA, ncB, ncC = _progs()
    I = inputs
    resA = run_bass_kernel_spmd(ncA, _prep_A(I), core_ids=list(range(8))).results
    xnfm, bsfm = [], []
    for b in range(2):
        xs = np.stack([np.asarray(resA[4 * b + j]["xn1f"]).reshape(128, 8, 2048)
                       for j in range(4)])
        xnfm.append(np.ascontiguousarray(
            xs.transpose(1, 2, 0, 3).reshape(128, 8 * 8192)))
        bs = np.stack([np.asarray(resA[4 * b + j]["basef"]).reshape(128, 8, 2048)
                       for j in range(4)])
        bsfm.append(np.ascontiguousarray(
            bs.transpose(1, 2, 0, 3).reshape(128, 8 * 8192)))
    resB = run_bass_kernel_spmd(ncB, _prep_B(I, xnfm, bsfm), core_ids=list(range(8))).results
    onrm_by_core = [resB[c]["onrm"] for c in range(8)]
    resC = run_bass_kernel_spmd(ncC, _prep_C(I, resA, onrm_by_core),
                                core_ids=list(range(8))).results
    y = np.zeros((2, 8192, 1024), np.float32)
    for core in range(8):
        b, j = core // 4, core % 4
        y[b, 2048 * j:2048 * (j + 1)] = (
            np.asarray(resC[core]["yf"], np.float32)
            .reshape(128, 8, 2048).transpose(1, 0, 2).reshape(1024, 2048).T)
    return y
